# version 06
# baseline (speedup 1.0000x reference)
import numpy as np
import ml_dtypes
from contextlib import ExitStack
import os as _os
import concourse.bass as bass
import concourse.mybir as mybir
from concourse.bass_utils import run_bass_kernel_spmd

SERIAL = set((_os.environ.get("MK_SERIAL") or "").split(","))

F32 = mybir.dt.float32
BF16 = mybir.dt.bfloat16
I32 = mybir.dt.int32
ALU = mybir.AluOpType
AFT = mybir.ActivationFunctionType
AXX = mybir.AxisListType.X

D = 2048
FF = 8192
ND = D // 128
NF = FF // 128
NSL = FF // 512
MAGIC = float(1.5 * 2**17)
HIC = float(127.0 / 64.0)
EPS = 1e-5
EXPMASK = 0x7F800000


class MK:
    def __init__(self, T=2048, debug=False):
        self.T = T
        self.debug = debug
        self.NT = T // 128
        nc = bass.Bass("TRN2", target_bir_lowering=False, detect_race_conditions=False)
        self.nc = nc
        self.es = ExitStack()
        dr = lambda n, sh, dt, k: nc.dram_tensor(n, sh, dt, kind=k).ap()
        self.x0 = dr("x0", [T, D], F32, "ExternalInput")
        self.watS = dr("watS", [2 * ND, 128, ND, 128], F32, "ExternalInput")
        self.wvS = dr("wvS", [4, 128, ND, 512], F32, "ExternalInput")
        self.woS = dr("woS", [4, 128, ND, 512], F32, "ExternalInput")
        self.w1 = dr("w1", [FF, D], F32, "ExternalInput")
        self.w2 = dr("w2", [D, FF], F32, "ExternalInput")
        self.lnw128 = dr("lnw128", [128, D], F32, "ExternalInput")
        self.lnb128 = dr("lnb128", [128, D], F32, "ExternalInput")
        self.b1v = dr("b1v", [1, FF], F32, "ExternalInput")
        self.b2v = dr("b2v", [1, D], F32, "ExternalInput")
        self.identd = dr("identd", [128, 128], BF16, "ExternalInput")
        self.ones2d = dr("ones2d", [2, 128], BF16, "ExternalInput")
        self.maskpd = dr("maskpd", [128, 128], F32, "ExternalInput")
        self.maskcd = dr("maskcd", [128, 128], F32, "ExternalInput")
        self.outd = dr("out", [T, D], F32, "ExternalOutput")
        dk = "ExternalOutput" if debug else "Internal"
        self.x1s = dr("x1s", [T, D], F32, dk)
        self.x2s = dr("x2s", [T, D], F32, dk)
        self.qw1t_s = dr("qw1t_s", [NSL, 128, ND, 512], BF16, "Internal")
        self.qw2t_s = dr("qw2t_s", [8, 128, NF, 256], BF16, "Internal")
        self.q2t_s = dr("q2t_s", [self.NT, 128, NF, 128], BF16, "Internal")
        self.qt_s = dr("qt_s", [self.NT, 128, ND, 128], BF16, dk)
        self.kt_s = dr("kt_s", [self.NT, 128, ND, 128], BF16, dk)
        self.v_s = dr("v_s", [T, D], BF16, dk)
        self.dbg_oT = dr("dbg_oT", [128, ND * T], BF16, dk) if debug else None
        self.S = {}
        self.n = {}
        for s in ("pe", "dv", "ac", "gp"):
            self.sem(s)
        sb = self.sb
        self.ident = sb("ident", [128, 128], BF16)
        self.ones2 = sb("ones2", [2, 128], BF16)
        self.b1hl = sb("b1hl", [2, FF], BF16)
        self.b2hl = sb("b2hl", [2, D], BF16)
        self.maskp = sb("maskp", [128, 128], F32)
        self.maskc = sb("maskc", [128, 128], F32)
        self.lnsave = sb("lnsave", [128, 2 * self.NT], F32)
        self.st24 = [sb(f"st24_{i}", [128, 24], F32) for i in range(2)]
        self.mv = [sb(f"mv_{i}", [128, 2], F32) for i in range(2)]
        self.vp = [sb(f"vp_{i}", [128, 1], F32) for i in range(2)]
        self.rt = [sb(f"rt_{i}", [128, 1], F32) for i in range(2)]
        self.tn = [sb(f"tn_{i}", [128, 1], F32) for i in range(2)]
        self.mv2 = [sb(f"mv2_{i}", [128, 2], F32) for i in range(2)]
        self.vp2 = [sb(f"vp2_{i}", [128, 1], F32) for i in range(2)]
        self.rt2 = [sb(f"rt2_{i}", [128, 1], F32) for i in range(2)]
        self.rs2 = [sb(f"rs2_{i}", [128, 1], F32) for i in range(2)]
        self.qam = [sb(f"qam_{i}", [128, 128], F32) for i in range(2)]
        self.qse = [sb(f"qse_{i}", [128, 128], F32) for i in range(2)]
        self.qmm = [sb(f"qmm_{i}", [128, 128], F32) for i in range(2)]
        self.qhi = [sb(f"qhi_{i}", [128, 128], F32) for i in range(2)]
        ps = lambda n: self.es.enter_context(nc.psum_tensor(n, [128, 512], F32))
        self.psA = ps("psA"); self.psB = ps("psB")
        self.psT1 = ps("psT1"); self.psT2 = ps("psT2")
        self.psS1 = ps("psS1"); self.psS2 = ps("psS2")
        self.psV1 = ps("psV1"); self.psV2 = ps("psV2")
        self.blk = self.es.enter_context(nc.Block())

    def sem(self, name):
        if name not in self.S:
            self.S[name] = self.es.enter_context(self.nc.semaphore("sem_" + name))
            self.n[name] = 0
        return self.S[name]

    def tb(self, ps, j):
        return ps[:, 64 * j:64 * (j + 1)].bitcast(BF16)

    def tball(self, ps):
        return ps[:, 0:256].bitcast(BF16).rearrange("p (j tt) -> p j tt", tt=128)

    def sb(self, name, shape, dt, es=None):
        self._uc = getattr(self, '_uc', 0) + 1
        return (es or self.es).enter_context(
            self.nc.sbuf_tensor(f"{name}_u{self._uc}", shape, dt))

    def inc(self, instr, sem, k=1, eng=None):
        self.sem(sem)
        if eng is not None:
            instr = eng.drain()
        instr.then_inc(self.S[sem], k)
        self.n[sem] += k
        return self.n[sem]

    def dma(self, eng, sem, out, in_):
        self.sem(sem)
        ins = eng.dma_start(out=out, in_=in_)
        return self.inc(ins, sem, 16)

    def quiesce(self):
        snap = dict(self.n)
        for en in ("sync", "scalar", "vector", "tensor", "gpsimd"):
            def f(e, snap=snap):
                for s, v in snap.items():
                    if v:
                        e.wait_ge(self.S[s], v)
            getattr(self.blk, en)(f)

    def p0(self):
        nt = {}
        with ExitStack() as es:
            b1f = self.sb("p0_b1f", [1, FF], F32, es)
            b2f = self.sb("p0_b2f", [1, D], F32, es)
            lof = self.sb("p0_lof", [1, FF], F32, es)

            def fsync(sq):
                self.dma(sq, 'cst', self.ident[:], self.identd[:])
                self.dma(sq, 'cst', self.ones2[:], self.ones2d[:])
                self.dma(sq, 'cst', self.maskp[:], self.maskpd[:])
                self.dma(sq, 'cst', self.maskc[:], self.maskcd[:])
                self.dma(sq, 'cst', b1f[:], self.b1v[:])
                nt['b'] = self.dma(sq, 'cst', b2f[:], self.b2v[:])
            self.blk.sync(fsync)

            lof2 = self.sb("p0_lof2", [1, D], F32, es)

            def fv(v):
                v.wait_ge(self.S['cst'], nt['b'])
                v.tensor_copy(self.b1hl[0:1, :], b1f[:])
                v.drain()
                v.tensor_tensor(lof[:], b1f[:], self.b1hl[0:1, :], ALU.subtract)
                v.tensor_copy(self.b2hl[0:1, :], b2f[:])
                v.drain()
                ins = v.tensor_tensor(lof2[:], b2f[:], self.b2hl[0:1, :], ALU.subtract)
                v.drain()
                nt['v'] = self.inc(ins, 'dv', eng=v)
            self.blk.vector(fv)

            def fg(g):
                g.wait_ge(self.S['dv'], nt['v'])
                self.dma(g, 'bhl', self.b1hl[1:2, :], lof[:])
                self.dma(g, 'bhl', self.b2hl[1:2, :], lof2[:])
            self.blk.gpsimd(fg)
        self.quiesce()

    def emit_ln_stats(self, src_ap, par, ln0, rs_out, tok_src_sem=None, tok_src=None):
        st24 = self.st24[par]
        mv = self.mv[par] if ln0 else self.mv2[par]
        vp = self.vp[par] if ln0 else self.vp2[par]
        rt = self.rt[par] if ln0 else self.rt2[par]
        tn = self.tn[par]
        nt = {}

        def fv(v):
            if tok_src is not None:
                v.wait_ge(self.S[tok_src_sem], tok_src)
            for c in range(4):
                v.bn_stats(st24[:, 6 * c:6 * c + 6], src_ap[:, 512 * c:512 * (c + 1)])
            v.drain()
            v.bn_aggr(mv[:], st24[:])
            v.drain()
            ins = v.tensor_scalar(vp[:], mv[:, 1:2], EPS, None, ALU.add)
            nt['v'] = self.inc(ins, 'dv', eng=v)
        self.blk.vector(fv)

        def fs(s):
            s.wait_ge(self.S['dv'], nt['v'])
            ins = s.activation(rt[:], vp[:], AFT.Sqrt)
            nt['a'] = self.inc(ins, 'ac', eng=s)
        self.blk.scalar(fs)

        def fv2(v):
            v.wait_ge(self.S['ac'], nt['a'])
            v.reciprocal(rs_out, rt[:])
            v.drain()
            v.tensor_tensor(tn[:], rs_out, rs_out, ALU.mult)
            v.drain()
            v.tensor_tensor(tn[:], tn[:], vp[:], ALU.mult)
            v.drain()
            v.tensor_scalar(tn[:], tn[:], -0.5, 1.5, ALU.mult, ALU.add)
            v.drain()
            ins = v.tensor_tensor(rs_out, rs_out, tn[:], ALU.mult)
            self.inc(ins, 'dv', eng=v)
        self.blk.vector(fv2)
        return mv

    def emit_quant(self, src, dst, tmp, n, par, split=False):
        split = False
        nb = n // 32
        am = self.qam[par][:, 0:nb]
        se = self.qse[par][:, 0:nb]
        mm = self.qmm[par][:, 0:nb]
        hi = self.qhi[par][:, 0:nb]
        bv = lambda ap: ap.rearrange("p (nb b) -> p nb b", b=32)
        mmb = mm.unsqueeze(2).broadcast_to([128, nb, 32])
        hib = hi.unsqueeze(2).broadcast_to([128, nb, 32])
        nt = {}

        def fv(v):
            v.drain()
            v.tensor_reduce(am, bv(src), AXX, ALU.max, apply_absolute_value=True)
            v.drain()
            v.tensor_scalar(se.bitcast(I32), am.bitcast(I32), EXPMASK, None,
                            ALU.bitwise_and)
            v.drain()
            v.tensor_scalar(mm, se, MAGIC, None, ALU.mult)
            ins = v.tensor_scalar(hi, se, HIC, None, ALU.mult)
            v.drain()
            nt['pre'] = self.inc(ins, 'dv', eng=v)
        self.blk.vector(fv)

        if split:
            def fg(g):
                g.wait_ge(self.S['dv'], nt['pre'])
                ins = g.tensor_tensor(bv(tmp), bv(src), mmb, ALU.add)
                nt['add'] = self.inc(ins, 'gp', eng=g)
            self.blk.gpsimd(fg)

            def fv2(v):
                v.wait_ge(self.S['gp'], nt['add'])
                ins = v.tensor_tensor(bv(tmp), bv(tmp), mmb, ALU.subtract)
                nt['sub'] = self.inc(ins, 'dv', eng=v)
            self.blk.vector(fv2)

            def fg2(g):
                g.wait_ge(self.S['dv'], nt['sub'])
                ins = g.tensor_tensor(bv(dst), bv(tmp), hib, ALU.min)
                nt['tok'] = self.inc(ins, 'gp', eng=g)
            self.blk.gpsimd(fg2)
            return ('gp', nt['tok'])
        else:
            def fv2(v):
                v.tensor_tensor(bv(tmp), bv(src), mmb, ALU.add)
                v.drain()
                v.tensor_tensor(bv(tmp), bv(tmp), mmb, ALU.subtract)
                v.drain()
                ins = v.tensor_tensor(bv(dst), bv(tmp), hib, ALU.min)
                v.drain()
                nt['tok'] = self.inc(ins, 'dv', eng=v)
            self.blk.vector(fv2)
            return ('dv', nt['tok'])

    def p_h0(self, x_in, q1T):
        NT = self.NT
        q1Tv = q1T[:].rearrange("p (c tt) -> p c tt", tt=self.T)
        with ExitStack() as es:
            lnw = self.sb("p1_lnw", [128, D], F32, es)
            lnb = self.sb("p1_lnb", [128, D], F32, es)
            xa = [self.sb(f"p1_xa{i}", [128, D], F32, es) for i in range(2)]
            xbt = [self.sb(f"p1_xb{i}", [128, D], BF16, es) for i in range(2)]
            ut = self.sb("p1_u", [128, D], F32, es)
            h0 = [self.sb(f"p1_h0{i}", [128, D], BF16, es) for i in range(2)]
            tmp = self.sb("p1_tmp", [128, D], F32, es)
            q1 = [self.sb(f"p1_q1{i}", [128, D], BF16, es) for i in range(2)]

            def fsync(sq):
                self.dma(sq, 'cst', lnw[:], self.lnw128[:])
                self.dma(sq, 'cst', lnb[:], self.lnb128[:])
            self.blk.sync(fsync)
            tok_cst = self.n['cst']

            guard_xa = [None, None]
            guard_xb = [None, None]
            guard_q1 = [None, None]
            evac_tok = [None, None]
            for t in range(NT):
                if 'p1' in SERIAL:
                    self.quiesce()
                par = t % 2

                def fsync2(sq, t=t, par=par):
                    if guard_xa[par] is not None:
                        sq.wait_ge(self.S['dv'], guard_xa[par])
                    self.dma(sq, f'xa{par}', xa[par][:],
                             x_in[128 * t:128 * (t + 1), :])
                self.blk.sync(fsync2)
                tok_x = self.n[f'xa{par}']

                self.emit_ln_stats(xa[par][:], par, True,
                                   self.lnsave[:, 2 * t:2 * t + 1],
                                   tok_src_sem=f'xa{par}', tok_src=tok_x)
                mv = self.mv[par]

                def fv(v, t=t, mv=mv):
                    v.drain()
                    ins = v.tensor_scalar(self.lnsave[:, 2 * t + 1:2 * t + 2],
                                          mv[:, 0:1], self.lnsave[:, 2 * t:2 * t + 1],
                                          -1.0, ALU.mult, ALU.mult)
                    v.drain()
                    self.inc(ins, 'dv', eng=v)
                self.blk.vector(fv)
                tok_nm = self.n['dv']

                def fs(s, t=t, par=par, tok_nm=tok_nm, gxb=guard_xb[par]):
                    s.wait_ge(self.S['dv'], tok_nm)
                    if gxb is not None:
                        s.wait_ge(self.S['dv'], gxb)
                    ins = s.activation(xbt[par][:], xa[par][:], AFT.Identity,
                                       bias=self.lnsave[:, 2 * t + 1:2 * t + 2],
                                       scale=self.lnsave[:, 2 * t:2 * t + 1])
                    self.inc(ins, 'ac', eng=s)
                self.blk.scalar(fs)
                tok_xb = self.n['ac']

                self.emit_ln_stats(xbt[par][:], par, False, self.rs2[par][:],
                                   tok_src_sem='ac', tok_src=tok_xb)
                mv2 = self.mv2[par]

                def fv2(v, par=par, mv2=mv2):
                    v.drain()
                    v.wait_ge(self.S['cst'], tok_cst)
                    v.tensor_scalar(ut[:], xbt[par][:], mv2[:, 0:1],
                                    self.rs2[par][:], ALU.subtract, ALU.mult)
                    v.drain()
                    v.tensor_tensor(ut[:], ut[:], lnw[:], ALU.mult)
                    v.drain()
                    ins = v.tensor_tensor(h0[par][:], ut[:], lnb[:], ALU.add)
                    self.inc(ins, 'dv', eng=v)
                self.blk.vector(fv2)
                guard_xa[par] = self.n['dv']
                guard_xb[par] = self.n['dv']

                def fvq(v, par=par, gq=guard_q1[par]):
                    if gq is not None:
                        v.wait_ge(self.S['pe'], gq)
                self.blk.vector(fvq)
                qsem, qtok = self.emit_quant(h0[par][:], q1[par][:], tmp[:], D, par)

                for grp in range(4):
                    pst = (self.psT1, self.psT2)[grp % 2]
                    gpar = grp % 2

                    def ft(te, par=par, grp=grp, pst=pst, qsem=qsem, qtok=qtok,
                           ev=evac_tok[grp % 2]):
                        te.wait_ge(self.S[qsem], qtok)
                        if ev is not None:
                            te.wait_ge(self.S['ac'], ev)
                        for j in range(4):
                            c = 4 * grp + j
                            ins = te.transpose(self.tb(pst, j),
                                               q1[par][:, 128 * c:128 * (c + 1)],
                                               self.ident[:])
                        self.inc(ins, 'pe', eng=te)
                    self.blk.tensor(ft)
                    tok_pe = self.n['pe']
                    if grp == 3:
                        guard_q1[par] = tok_pe

                    def fse(s, t=t, grp=grp, pst=pst, tok_pe=tok_pe):
                        s.wait_ge(self.S['pe'], tok_pe)
                        ins = s.activation(
                            q1Tv[:, 4 * grp:4 * grp + 4, 128 * t:128 * (t + 1)],
                            self.tball(pst), AFT.Copy)
                        self.inc(ins, 'ac', eng=s)
                    self.blk.scalar(fse)
                    evac_tok[gpar] = self.n['ac']
        self.quiesce()

    def p_m1(self, first, q1T):
        NT = self.NT
        q1Tv = q1T[:].rearrange("p (c tt) -> p c tt", tt=self.T)
        with ExitStack() as es:
            wn = [self.sb(f"p2_wn{i}", [128, D], F32, es) for i in range(2)]
            qwn = [self.sb(f"p2_qwn{i}", [128, D], BF16, es) for i in range(2)]
            wtmp = [self.sb(f"p2_wtmp{i}", [128, D], F32, es) for i in range(2)]
            wbuf = [self.sb(f"p2_wbuf{i}", [128, ND * 512], BF16, es) for i in range(2)]
            w2st = [self.sb(f"p2_w2st{i}", [128, 4 * 128], BF16, es) for i in range(2)]
            gsb = [self.sb(f"p2_g{i}", [128, 512], BF16, es) for i in range(2)]
            q2p = [self.sb(f"p2_q2{i}", [128, 512], BF16, es) for i in range(2)]
            t2f = [self.sb(f"p2_t2f{i}", [128, 512], F32, es) for i in range(2)]
            q2st = [self.sb(f"p2_q2st{i}", [128, 4 * 128], BF16, es) for i in range(2)]
            wbufv = [w[:].rearrange("p (c o) -> p c o", o=512) for w in wbuf]

            wq_done = [None, None]
            mm_done = [None, None]
            st_done = [None, None]
            w2st_done = [None, None]
            g_guard = [None, None]
            q2_guard = [None, None]
            wn_guard = [None, None]
            qwn_guard = [None, None]
            psmm_guard = [None, None]
            pst2_guard = [None]
            pst1_guard = [None]
            wcount = [0]

            def w1_rt(sl, rt):
                bpar = sl % 2
                wpar = wcount[0] % 2
                wcount[0] += 1

                def fsync(sq, sl=sl, rt=rt, wpar=wpar, g=wn_guard[wpar]):
                    if g is not None:
                        sq.wait_ge(self.S[g[0]], g[1])
                    r0 = 512 * sl + 128 * rt
                    self.dma(sq, f'wn{wpar}', wn[wpar][:], self.w1[r0:r0 + 128, :])
                self.blk.sync(fsync)
                tok_ld = self.n[f'wn{wpar}']

                def fv(v, tok_ld=tok_ld, wpar=wpar, g=qwn_guard[wpar]):
                    v.wait_ge(self.S[f'wn{wpar}'], tok_ld)
                    if g is not None:
                        v.wait_ge(self.S['pe'], g)
                self.blk.vector(fv)
                qsem, qtok = self.emit_quant(wn[wpar][:], qwn[wpar][:],
                                             wtmp[wpar][:], D, wpar, split=False)
                wn_guard[wpar] = (qsem, qtok)
                for grp in range(4):
                    def ft(te, grp=grp, wpar=wpar, qsem=qsem, qtok=qtok,
                           g=pst2_guard[0]):
                        te.wait_ge(self.S[qsem], qtok)
                        if g is not None:
                            te.wait_ge(self.S['ac'], g)
                        for j in range(4):
                            c = 4 * grp + j
                            ins = te.transpose(self.tb(self.psT2, j),
                                               qwn[wpar][:, 128 * c:128 * (c + 1)],
                                               self.ident[:])
                        self.inc(ins, 'pe', eng=te)
                    self.blk.tensor(ft)
                    tok_pe = self.n['pe']
                    if grp == 3:
                        qwn_guard[wpar] = tok_pe

                    def fse(s, rt=rt, grp=grp, bpar=bpar, tok_pe=tok_pe,
                            mmd=mm_done[bpar]):
                        s.wait_ge(self.S['pe'], tok_pe)
                        if mmd is not None and rt == 0 and grp == 0:
                            s.wait_ge(self.S['pe'], mmd)
                        ins = s.activation(
                            wbufv[bpar][:, 4 * grp:4 * grp + 4,
                                        128 * rt:128 * (rt + 1)],
                            self.tball(self.psT2), AFT.Copy)
                        self.inc(ins, 'ac', eng=s)
                    self.blk.scalar(fse)
                    pst2_guard[0] = self.n['ac']

            def w1_store(sl):
                bpar = sl % 2
                tok = self.n['ac']

                def fst(sp, sl=sl, bpar=bpar, tok=tok):
                    sp.wait_ge(self.S['ac'], tok)
                    self.dma(sp, f'qws{bpar}', self.qw1t_s[sl], wbufv[bpar])
                self.blk.scalar(fst)

            def w1_load(sl):
                bpar = sl % 2

                def fsync(sq, sl=sl, bpar=bpar, mmd=mm_done[bpar]):
                    if mmd is not None:
                        sq.wait_ge(self.S['pe'], mmd)
                    tok = self.dma(sq, f'wb{bpar}', wbufv[bpar], self.qw1t_s[sl])
                    wq_done[bpar] = (f'wb{bpar}', tok)
                self.blk.sync(fsync)

            def w2_qt(qi):
                rti, kq = qi // 4, qi % 4
                wpar = wcount[0] % 2
                wcount[0] += 1

                def fsync(sq, rti=rti, kq=kq, wpar=wpar, g=wn_guard[wpar]):
                    if g is not None:
                        sq.wait_ge(self.S[g[0]], g[1])
                    self.dma(sq, f'wn{wpar}', wn[wpar][:],
                             self.w2[128 * rti:128 * (rti + 1),
                                     2048 * kq:2048 * (kq + 1)])
                self.blk.sync(fsync)
                tok_ld = self.n[f'wn{wpar}']

                def fv(v, tok_ld=tok_ld, wpar=wpar, g=qwn_guard[wpar]):
                    v.wait_ge(self.S[f'wn{wpar}'], tok_ld)
                    if g is not None:
                        v.wait_ge(self.S['pe'], g)
                self.blk.vector(fv)
                qsem, qtok = self.emit_quant(wn[wpar][:], qwn[wpar][:],
                                             wtmp[wpar][:], D, wpar, split=False)
                wn_guard[wpar] = (qsem, qtok)
                for grp in range(4):
                    spar = grp % 2

                    def ft(te, grp=grp, wpar=wpar, qsem=qsem, qtok=qtok,
                           g=pst2_guard[0]):
                        te.wait_ge(self.S[qsem], qtok)
                        if g is not None:
                            te.wait_ge(self.S['ac'], g)
                        for j in range(4):
                            c = 4 * grp + j
                            ins = te.transpose(self.tb(self.psT2, j),
                                               qwn[wpar][:, 128 * c:128 * (c + 1)],
                                               self.ident[:])
                        self.inc(ins, 'pe', eng=te)
                    self.blk.tensor(ft)
                    tok_pe = self.n['pe']
                    if grp == 3:
                        qwn_guard[wpar] = tok_pe

                    def fse(s, grp=grp, spar=spar, tok_pe=tok_pe,
                            g=w2st_done[spar]):
                        s.wait_ge(self.S['pe'], tok_pe)
                        if g is not None:
                            s.wait_ge(self.S[f'w2s{spar}'], g)
                        ins = s.activation(
                            w2st[spar][:].rearrange("p (j d) -> p j d", d=128),
                            self.tball(self.psT2), AFT.Copy)
                        self.inc(ins, 'ac', eng=s)
                    self.blk.scalar(fse)
                    pst2_guard[0] = self.n['ac']
                    tok_ev = self.n['ac']

                    def fst(sp, rti=rti, kq=kq, grp=grp, spar=spar, tok_ev=tok_ev):
                        sp.wait_ge(self.S['ac'], tok_ev)
                        H = rti // 2
                        c0 = 16 * kq + 4 * grp
                        dof = 128 * (rti % 2)
                        tok = self.dma(
                            sp, f'w2s{spar}',
                            self.qw2t_s[H, :, c0:c0 + 4, dof:dof + 128],
                            w2st[spar][:].rearrange("p (j d) -> p j d", d=128))
                        w2st_done[spar] = tok
                    self.blk.scalar(fst)

            if first:
                for rt in range(4):
                    w1_rt(0, rt)
                wq_done[0] = ('ac', self.n['ac'])
                w1_store(0)
            else:
                w1_load(0)

            nw2 = [0]

            def emit_witem(it):
                kind, a = it
                if kind == 'w1':
                    w1_rt(a[0], a[1])
                    if a[1] == 3:
                        wq_done[a[0] % 2] = ('ac', self.n['ac'])
                        w1_store(a[0])
                else:
                    w2_qt(a)

            for sl in range(NSL):
                bpar = sl % 2
                witems = []
                if first:
                    if sl + 1 < NSL:
                        witems += [('w1', (sl + 1, rt)) for rt in range(4)]
                    nav = min(4 if sl + 1 < NSL else 64 - nw2[0], 64 - nw2[0])
                    witems += [('w2', nw2[0] + i) for i in range(nav)]
                    nw2[0] += nav
                elif sl + 1 < NSL:
                    pass
                done = 0
                for t in range(NT):
                    if 'p2' in SERIAL:
                        self.quiesce()
                    want = ((t + 1) * len(witems) + NT - 1) // NT
                    while done < min(want, len(witems)):
                        emit_witem(witems[done])
                        done += 1
                    if (not first) and sl + 1 < NSL and t == 0:
                        w1_load(sl + 1)
                    wsem, wtok = wq_done[bpar]
                    pmm = (self.psA, self.psB)[t % 2]
                    ppar = t % 2

                    def fmm(te, sl=sl, t=t, pmm=pmm, ppar=ppar, wsem=wsem,
                            wtok=wtok, g=psmm_guard[ppar]):
                        te.wait_ge(self.S[wsem], wtok)
                        if g is not None:
                            te.wait_ge(self.S['ac'], g)
                        for c in range(ND):
                            te.matmul(pmm[:], q1Tv[:, c, 128 * t:128 * (t + 1)],
                                      wbufv[sl % 2][:, c, :],
                                      start=(c == 0), stop=False)
                        ins = te.matmul(pmm[:], self.ones2[:],
                                        self.b1hl[:, 512 * sl:512 * (sl + 1)],
                                        start=False, stop=True)
                        self.inc(ins, 'pe', eng=te)
                    self.blk.tensor(fmm)
                    tok_mm = self.n['pe']
                    if t == NT - 1:
                        mm_done[bpar] = tok_mm

                    def fgelu(s, pmm=pmm, ppar=ppar, tok_mm=tok_mm,
                              g=g_guard[ppar]):
                        s.wait_ge(self.S['pe'], tok_mm)
                        if g is not None:
                            s.wait_ge(self.S['dv'], g)
                        ins = s.activation(gsb[ppar][:], pmm[:], AFT.Gelu_apprx_tanh)
                        self.inc(ins, 'ac', eng=s)
                    self.blk.scalar(fgelu)
                    tok_g = self.n['ac']
                    psmm_guard[ppar] = tok_g

                    def fvw(v, ppar=ppar, tok_g=tok_g, g=q2_guard[ppar]):
                        v.wait_ge(self.S['ac'], tok_g)
                        if g is not None:
                            v.wait_ge(self.S['pe'], g)
                    self.blk.vector(fvw)
                    qsem, qtok = self.emit_quant(gsb[ppar][:], q2p[ppar][:],
                                                 t2f[ppar][:], 512, ppar)
                    g_guard[ppar] = self.n['dv']

                    def ftq(te, ppar=ppar, qsem=qsem, qtok=qtok, g=pst1_guard[0]):
                        te.wait_ge(self.S[qsem], qtok)
                        if g is not None:
                            te.wait_ge(self.S['dv'], g)
                        for j in range(4):
                            ins = te.transpose(self.tb(self.psT1, j),
                                               q2p[ppar][:, 128 * j:128 * (j + 1)],
                                               self.ident[:])
                        self.inc(ins, 'pe', eng=te)
                    self.blk.tensor(ftq)
                    tok_tq = self.n['pe']
                    q2_guard[ppar] = tok_tq

                    def fve(v, ppar=ppar, tok_tq=tok_tq, g=st_done[ppar]):
                        v.wait_ge(self.S['pe'], tok_tq)
                        if g is not None:
                            v.wait_ge(self.S[f'q2s{ppar}'], g)
                        ins = v.tensor_copy(
                            q2st[ppar][:].rearrange("p (j tt) -> p j tt", tt=128),
                            self.tball(self.psT1))
                        self.inc(ins, 'dv', eng=v)
                    self.blk.vector(fve)
                    pst1_guard[0] = self.n['dv']
                    tok_ev = self.n['dv']

                    def fst(sp, sl=sl, t=t, ppar=ppar, tok_ev=tok_ev):
                        sp.wait_ge(self.S['dv'], tok_ev)
                        tok = self.dma(sp, f'q2s{ppar}',
                                       self.q2t_s[t, :, 4 * sl:4 * (sl + 1), :],
                                       q2st[ppar][:].rearrange(
                                           "p (j tt) -> p j tt", tt=128))
                        st_done[ppar] = tok
                    self.blk.scalar(fst)
                while done < len(witems):
                    emit_witem(witems[done])
                    done += 1
        self.quiesce()

    def p_m2(self, x_in, x_out, out_sem):
        NT = self.NT
        with ExitStack() as es:
            ring = [self.sb(f"p3_ring{i}", [128, NF * 256], BF16, es) for i in range(3)]
            ringv = [r[:].rearrange("p (c d) -> p c d", d=256) for r in ring]
            q2tb = [self.sb(f"p3_q2t{i}", [128, NF * 128], BF16, es) for i in range(2)]
            q2tv = [r[:].rearrange("p (c tt) -> p c tt", tt=128) for r in q2tb]
            xpc = [self.sb(f"p3_xp{i}", [128, 512], F32, es) for i in range(2)]
            xbp = [self.sb(f"p3_xbp{i}", [128, 512], BF16, es) for i in range(2)]
            h2p = [self.sb(f"p3_h2p{i}", [128, 512], BF16, es) for i in range(2)]
            up = [self.sb(f"p3_up{i}", [128, 512], BF16, es) for i in range(2)]
            xo = [self.sb(f"p3_xo{i}", [128, 512], F32, es) for i in range(2)]

            ring_ready = [None] * 3
            ring_guard = [None] * 3
            q2_ready = [None, None]
            q2_guard = [None, None]
            psmm_guard = [None] * 4
            xo_guard = [None, None]
            xpc_guard = [None, None]
            xbp_guard = [None, None]
            h2_guard = [None, None]

            def build_half(H):
                slot = H % 3

                def fsync(sq, H=H, slot=slot, g=ring_guard[slot]):
                    if g is not None:
                        sq.wait_ge(self.S['pe'], g)
                    tok = self.dma(sq, f'rg{slot}', ringv[slot], self.qw2t_s[H])
                    ring_ready[slot] = (f'rg{slot}', tok)
                self.blk.sync(fsync)

            def load_q2t(t):
                par = t % 2

                def fsync(sq, t=t, par=par, g=q2_guard[par]):
                    if g is not None:
                        sq.wait_ge(self.S['pe'], g)
                    tok = self.dma(sq, f'q2l{par}', q2tv[par], self.q2t_s[t])
                    q2_ready[par] = tok
                self.blk.sync(fsync)

            build_half(0)
            build_half(1)
            load_q2t(0)
            for sg in range(4):
                lo, hi = (2 * sg) % 3, (2 * sg + 1) % 3
                for t in range(NT):
                    if 'p3' in SERIAL:
                        self.quiesce()
                    if t == 0 and 2 * sg + 2 < 8:
                        build_half(2 * sg + 2)
                    if t + 1 < NT:
                        load_q2t(t + 1)
                    elif sg + 1 < 4:
                        load_q2t(0)
                    par = t % 2
                    pa = (self.psA, self.psS1)[par]
                    pb = (self.psB, self.psS2)[par]
                    ia, ib = (0, 1) if par == 0 else (2, 3)
                    rlo, rhi = ring_ready[lo], ring_ready[hi]

                    def fmm(te, sg=sg, t=t, par=par, pa=pa, pb=pb, lo=lo, hi=hi,
                            rlo=rlo, rhi=rhi, q2r=q2_ready[par],
                            ga=psmm_guard[ia], gb=psmm_guard[ib]):
                        te.wait_ge(self.S[rlo[0]], rlo[1])
                        te.wait_ge(self.S[rhi[0]], rhi[1])
                        te.wait_ge(self.S[f'q2l{par}'], q2r)
                        if ga is not None:
                            te.wait_ge(self.S['dv'], ga)
                        if gb is not None:
                            te.wait_ge(self.S['dv'], gb)
                        for c in range(NF):
                            te.matmul(pa[:, 0:256], q2tv[par][:, c, :],
                                      ringv[lo][:, c, :], start=(c == 0), stop=False)
                            te.matmul(pb[:, 0:256], q2tv[par][:, c, :],
                                      ringv[hi][:, c, :], start=(c == 0), stop=False)
                        te.matmul(pa[:, 0:256], self.ones2[:],
                                  self.b2hl[:, 512 * sg:512 * sg + 256],
                                  start=False, stop=True)
                        ins = te.matmul(pb[:, 0:256], self.ones2[:],
                                        self.b2hl[:, 512 * sg + 256:512 * (sg + 1)],
                                        start=False, stop=True)
                        self.inc(ins, 'pe', eng=te)
                    self.blk.tensor(fmm)
                    tok_mm = self.n['pe']
                    q2_guard[par] = tok_mm
                    if t == NT - 1:
                        ring_guard[lo] = tok_mm
                        ring_guard[hi] = tok_mm

                    def fsx(sq, sg=sg, t=t, par=par, g=xpc_guard[par]):
                        if g is not None:
                            sq.wait_ge(self.S['dv'], g)
                        self.dma(sq, f'xp{par}', xpc[par][:],
                                 x_in[128 * t:128 * (t + 1), 512 * sg:512 * (sg + 1)])
                    self.blk.sync(fsx)
                    tok_xp = self.n[f'xp{par}']

                    def fac(s, t=t, par=par, pa=pa, pb=pb, tok_mm=tok_mm,
                            tok_xp=tok_xp, gh=h2_guard[par], gx=xbp_guard[par]):
                        s.wait_ge(self.S['pe'], tok_mm)
                        s.wait_ge(self.S[f'xp{par}'], tok_xp)
                        if gh is not None:
                            s.wait_ge(self.S['dv'], gh)
                        s.activation(h2p[par][:, 0:256], pa[:, 0:256], AFT.Copy)
                        s.activation(h2p[par][:, 256:512], pb[:, 0:256], AFT.Copy)
                        if gx is not None:
                            s.wait_ge(self.S['dv'], gx)
                        ins = s.activation(xbp[par][:], xpc[par][:], AFT.Identity,
                                           bias=self.lnsave[:, 2 * t + 1:2 * t + 2],
                                           scale=self.lnsave[:, 2 * t:2 * t + 1])
                        self.inc(ins, 'ac', eng=s)
                    self.blk.scalar(fac)
                    tok_h2 = self.n['ac']

                    def fvu(v, par=par, tok_h2=tok_h2, g=xo_guard[par]):
                        v.wait_ge(self.S['ac'], tok_h2)
                        v.tensor_tensor(up[par][:], xbp[par][:], h2p[par][:], ALU.add)
                        v.drain()
                        if g is not None:
                            v.wait_ge(self.S[f'{out_sem}{par}'], g)
                        ins = v.scalar_tensor_tensor(xo[par][:], up[par][:], 0.5,
                                                     xpc[par][:], ALU.mult, ALU.add)
                        v.drain()
                        self.inc(ins, 'dv', eng=v)
                    self.blk.vector(fvu)
                    tok_u = self.n['dv']
                    psmm_guard[ia] = tok_u
                    psmm_guard[ib] = tok_u
                    xpc_guard[par] = tok_u
                    xbp_guard[par] = tok_u
                    h2_guard[par] = tok_u

                    def fso(sp, sg=sg, t=t, par=par, tok_u=tok_u):
                        sp.wait_ge(self.S['dv'], tok_u)
                        tok = self.dma(sp, f'{out_sem}{par}',
                                       x_out[128 * t:128 * (t + 1),
                                             512 * sg:512 * (sg + 1)],
                                       xo[par][:])
                        xo_guard[par] = tok
                    self.blk.scalar(fso)
                if 2 * sg + 3 < 8:
                    build_half(2 * sg + 3)
        self.quiesce()

    def p_qkv(self, x_in):
        NT, T = self.NT, self.T
        with ExitStack() as es:
            aT = self.sb("p4_aT", [128, ND * T], BF16, es)
            aTv = aT[:].rearrange("p (c tt) -> p c tt", tt=T)
            xa = [self.sb(f"p4_xa{i}", [128, D], F32, es) for i in range(2)]
            at = [self.sb(f"p4_a{i}", [128, D], BF16, es) for i in range(2)]
            wqb = [self.sb(f"p4_wq{i}", [128, ND * 128], BF16, es) for i in range(2)]
            wvb = [self.sb(f"p4_wv{i}", [128, ND * 512], BF16, es) for i in range(2)]
            qkst = [self.sb(f"p4_qkst{i}", [128, 512], BF16, es) for i in range(2)]
            vst = [self.sb(f"p4_vst{i}", [128, 512], BF16, es) for i in range(2)]
            wqv = [w[:].rearrange("p (c j) -> p c j", j=128) for w in wqb]
            wvv = [w[:].rearrange("p (c o) -> p c o", o=512) for w in wvb]

            guard_xa = [None, None]
            guard_a = [None, None]
            evac_tok = [None, None]
            for t in range(NT):
                if 'p4' in SERIAL:
                    self.quiesce()
                par = t % 2

                def fsync(sq, t=t, par=par, g=guard_xa[par]):
                    if g is not None:
                        sq.wait_ge(self.S['ac'], g)
                    self.dma(sq, f'xa{par}', xa[par][:],
                             x_in[128 * t:128 * (t + 1), :])
                self.blk.sync(fsync)
                tok_x = self.n[f'xa{par}']
                self.emit_ln_stats(xa[par][:], par, True,
                                   self.lnsave[:, 2 * t:2 * t + 1],
                                   tok_src_sem=f'xa{par}', tok_src=tok_x)
                mv = self.mv[par]

                def fv(v, t=t, mv=mv):
                    v.drain()
                    ins = v.tensor_scalar(self.lnsave[:, 2 * t + 1:2 * t + 2],
                                          mv[:, 0:1], self.lnsave[:, 2 * t:2 * t + 1],
                                          -1.0, ALU.mult, ALU.mult)
                    v.drain()
                    self.inc(ins, 'dv', eng=v)
                self.blk.vector(fv)
                tok_nm = self.n['dv']

                def fs(s, t=t, par=par, tok_nm=tok_nm, g=guard_a[par]):
                    s.wait_ge(self.S['dv'], tok_nm)
                    if g is not None:
                        s.wait_ge(self.S['pe'], g)
                    ins = s.activation(at[par][:], xa[par][:], AFT.Identity,
                                       bias=self.lnsave[:, 2 * t + 1:2 * t + 2],
                                       scale=self.lnsave[:, 2 * t:2 * t + 1])
                    self.inc(ins, 'ac', eng=s)
                self.blk.scalar(fs)
                guard_xa[par] = self.n['ac']
                tok_a = self.n['ac']

                for grp in range(4):
                    pst = (self.psT1, self.psT2)[grp % 2]
                    gpar = grp % 2

                    def ft(te, par=par, grp=grp, pst=pst, tok_a=tok_a,
                           ev=evac_tok[grp % 2]):
                        te.wait_ge(self.S['ac'], tok_a)
                        if ev is not None:
                            te.wait_ge(self.S['dv'], ev)
                        for j in range(4):
                            c = 4 * grp + j
                            ins = te.transpose(self.tb(pst, j),
                                               at[par][:, 128 * c:128 * (c + 1)],
                                               self.ident[:])
                        self.inc(ins, 'pe', eng=te)
                    self.blk.tensor(ft)
                    tok_pe = self.n['pe']
                    if grp == 3:
                        guard_a[par] = tok_pe

                    def fve(v, t=t, grp=grp, pst=pst, tok_pe=tok_pe):
                        v.wait_ge(self.S['pe'], tok_pe)
                        ins = v.tensor_copy(
                            aTv[:, 4 * grp:4 * grp + 4, 128 * t:128 * (t + 1)],
                            self.tball(pst))
                        self.inc(ins, 'dv', eng=v)
                    self.blk.vector(fve)
                    evac_tok[gpar] = self.n['dv']
            tok_aT = self.n['dv']

            wq_tok = [None, None]
            wq_guard = [None, None]
            st_guard = [None, None]
            psg = [None, None]

            def loadwq(oc):
                par = oc % 2

                def fg(g, oc=oc, par=par, gg=wq_guard[par]):
                    if gg is not None:
                        g.wait_ge(self.S['pe'], gg)
                    wq_tok[par] = self.dma(g, f'wq{par}', wqv[par], self.watS[oc])
                self.blk.gpsimd(fg)

            loadwq(0)
            for oc in range(2 * ND):
                if 'p4' in SERIAL:
                    self.quiesce()
                par = oc % 2
                if oc + 1 < 2 * ND:
                    loadwq(oc + 1)
                isq = oc < ND
                for ts4 in range(T // 512):
                    pmm = (self.psA, self.psB)[ts4 % 2]
                    ppar = ts4 % 2

                    def fmm(te, oc=oc, ts4=ts4, pmm=pmm, ppar=ppar, par=par,
                            wt=wq_tok[par], g=psg[ppar]):
                        te.wait_ge(self.S[f'wq{par}'], wt)
                        if oc == 0 and ts4 == 0:
                            te.wait_ge(self.S['dv'], tok_aT)
                        if g is not None:
                            te.wait_ge(self.S['ac'], g)
                        for c in range(ND):
                            ins = te.matmul(pmm[:], wqv[par][:, c, :],
                                            aTv[:, c, 512 * ts4:512 * (ts4 + 1)],
                                            start=(c == 0), stop=(c == ND - 1))
                        self.inc(ins, 'pe', eng=te)
                    self.blk.tensor(fmm)
                    tok_mm = self.n['pe']
                    if ts4 == T // 512 - 1:
                        wq_guard[par] = tok_mm

                    def fse(s, pmm=pmm, ppar=ppar, isq=isq, tok_mm=tok_mm,
                            g=st_guard[ppar]):
                        s.wait_ge(self.S['pe'], tok_mm)
                        if g is not None:
                            s.wait_ge(self.S[f'qk{ppar}'], g)
                        ins = s.activation(qkst[ppar][:], pmm[:], AFT.Copy,
                                           scale=(1.0 / 128.0) if isq else 1.0)
                        self.inc(ins, 'ac', eng=s)
                    self.blk.scalar(fse)
                    tok_ev = self.n['ac']
                    psg[ppar] = tok_ev

                    def fst(sp, oc=oc, ts4=ts4, ppar=ppar, isq=isq, tok_ev=tok_ev):
                        sp.wait_ge(self.S['ac'], tok_ev)
                        dst = self.qt_s if isq else self.kt_s
                        occ = oc if isq else oc - ND
                        tok = self.dma(sp, f'qk{ppar}',
                                       dst[4 * ts4:4 * ts4 + 4, :, occ, :].rearrange(
                                           "tb p j -> p tb j"),
                                       qkst[ppar][:].rearrange(
                                           "p (tb j) -> p tb j", j=128))
                        st_guard[ppar] = tok
                    self.blk.scalar(fst)

            wv_tok = [None, None]
            wv_guard = [None, None]
            vst_guard = [None, None]
            psgv = [self.n['ac'], self.n['ac']]

            def loadwv(os):
                par = os % 2

                def fg(g, os=os, par=par, gg=wv_guard[par]):
                    if gg is not None:
                        g.wait_ge(self.S['pe'], gg)
                    wv_tok[par] = self.dma(g, f'wv{par}', wvv[par], self.wvS[os])
                self.blk.gpsimd(fg)

            loadwv(0)
            for os in range(4):
                if 'p4' in SERIAL:
                    self.quiesce()
                par = os % 2
                if os + 1 < 4:
                    loadwv(os + 1)
                for t in range(NT):
                    pmm = (self.psA, self.psB)[t % 2]
                    ppar = t % 2

                    def fmm(te, os=os, t=t, pmm=pmm, ppar=ppar, par=par,
                            wt=wv_tok[par], g=psgv[ppar]):
                        te.wait_ge(self.S[f'wv{par}'], wt)
                        if g is not None:
                            te.wait_ge(self.S['ac'], g)
                        for c in range(ND):
                            ins = te.matmul(pmm[:], aTv[:, c, 128 * t:128 * (t + 1)],
                                            wvv[par][:, c, :],
                                            start=(c == 0), stop=(c == ND - 1))
                        self.inc(ins, 'pe', eng=te)
                    self.blk.tensor(fmm)
                    tok_mm = self.n['pe']
                    if t == NT - 1:
                        wv_guard[par] = tok_mm

                    def fse(s, pmm=pmm, ppar=ppar, tok_mm=tok_mm, g=vst_guard[ppar]):
                        s.wait_ge(self.S['pe'], tok_mm)
                        if g is not None:
                            s.wait_ge(self.S[f'vs{ppar}'], g)
                        ins = s.activation(vst[ppar][:], pmm[:], AFT.Copy)
                        self.inc(ins, 'ac', eng=s)
                    self.blk.scalar(fse)
                    tok_ev = self.n['ac']
                    psgv[ppar] = tok_ev

                    def fst(sp, os=os, t=t, ppar=ppar, tok_ev=tok_ev):
                        sp.wait_ge(self.S['ac'], tok_ev)
                        tok = self.dma(sp, f'vs{ppar}',
                                       self.v_s[128 * t:128 * (t + 1),
                                                512 * os:512 * (os + 1)],
                                       vst[ppar][:])
                        vst_guard[ppar] = tok
                    self.blk.scalar(fst)
        self.quiesce()

    def p_attn(self, x_in, x_out):
        NT, T = self.NT, self.T
        with ExitStack() as es:
            oT = self.sb("p5_oT", [128, ND * T], BF16, es)
            oTv = oT[:].rearrange("p (c tt) -> p c tt", tt=T)
            ktb = [self.sb(f"p5_kt{i}", [128, ND * 128], BF16, es) for i in range(2)]
            qtb = [self.sb(f"p5_qt{i}", [128, ND * 128], BF16, es) for i in range(2)]
            vb = [self.sb(f"p5_v{i}", [128, D], BF16, es) for i in range(2)]
            rp = self.sb("p5_rp", [128, 128], F32, es)
            rc = self.sb("p5_rc", [128, 128], F32, es)
            tsc = self.sb("p5_tsc", [128, 128], F32, es)
            atp = self.sb("p5_atp", [128, 128], BF16, es)
            atc = self.sb("p5_atc", [128, 128], BF16, es)
            ktv = [k[:].rearrange("p (c j) -> p c j", j=128) for k in ktb]
            qtv = [k[:].rearrange("p (c j) -> p c j", j=128) for k in qtb]

            kt_tok = [None, None]
            qt_tok = [None, None]
            v_tok = [None, None]
            last_av = [None]
            relu_guard = [None]
            evac_guard = [None, None]

            def loadw(w):
                par = w % 2

                def fq(sq, w=w, par=par, g=last_av[0]):
                    if g is not None:
                        sq.wait_ge(self.S['pe'], g)
                    self.dma(sq, f'kv{par}', ktv[par], self.kt_s[w])
                    self.dma(sq, f'kv{par}', qtv[par], self.qt_s[w])
                    kt_tok[par] = qt_tok[par] = v_tok[par] = self.dma(
                        sq, f'kv{par}', vb[par][:],
                        self.v_s[128 * w:128 * (w + 1), :])
                self.blk.sync(fq)

            loadw(0)
            for w in range(NT):
                if 'p5' in SERIAL:
                    self.quiesce()
                par = w % 2
                havep = w > 0

                def fsim(te, w=w, par=par, havep=havep, rg=relu_guard[0],
                         qt=qt_tok[par], kt=kt_tok[par]):
                    te.wait_ge(self.S[f'kv{par}'], qt)
                    if rg is not None:
                        te.wait_ge(self.S['ac'], rg)
                    if havep:
                        for c in range(ND):
                            te.matmul(self.psS1[:, 0:128], ktv[1 - par][:, c, :],
                                      qtv[par][:, c, :], start=(c == 0),
                                      stop=(c == ND - 1))
                    for c in range(ND):
                        ins = te.matmul(self.psS2[:, 0:128], ktv[par][:, c, :],
                                        qtv[par][:, c, :], start=(c == 0),
                                        stop=(c == ND - 1))
                    self.inc(ins, 'pe', eng=te)
                self.blk.tensor(fsim)
                tok_sim = self.n['pe']

                def frelu(s, havep=havep, tok_sim=tok_sim, ag=last_av[0]):
                    s.wait_ge(self.S['pe'], tok_sim)
                    if ag is not None:
                        s.wait_ge(self.S['pe'], ag)
                    if havep:
                        s.activation(rp[:], self.psS1[:, 0:128], AFT.Relu)
                    ins = s.activation(rc[:], self.psS2[:, 0:128], AFT.Relu)
                    self.inc(ins, 'ac', eng=s)
                self.blk.scalar(frelu)
                relu_guard[0] = self.n['ac']
                tok_relu = self.n['ac']

                def fmask(v, havep=havep, tok_relu=tok_relu):
                    v.wait_ge(self.S['ac'], tok_relu)
                    if havep:
                        v.tensor_tensor(tsc[:], rp[:], self.maskp[:], ALU.mult)
                        v.drain()
                        v.tensor_tensor(atp[:], tsc[:], rp[:], ALU.mult)
                        v.drain()
                    v.tensor_tensor(tsc[:], rc[:], self.maskc[:], ALU.mult)
                    v.drain()
                    ins = v.tensor_tensor(atc[:], tsc[:], rc[:], ALU.mult)
                    v.drain()
                    self.inc(ins, 'dv', eng=v)
                self.blk.vector(fmask)
                tok_at = self.n['dv']

                for grp in range(4):
                    psv = (self.psV1, self.psV2)[grp % 2]
                    gpar = grp % 2

                    def fav(te, par=par, havep=havep, grp=grp, psv=psv,
                            tok_at=tok_at, vt=v_tok[par], ev=evac_guard[grp % 2]):
                        te.wait_ge(self.S['dv'], tok_at)
                        te.wait_ge(self.S[f'kv{par}'], vt)
                        if ev is not None:
                            te.wait_ge(self.S['dv'], ev)
                        for j in range(4):
                            c = 4 * grp + j
                            if havep:
                                te.matmul(psv[:, 128 * j:128 * (j + 1)],
                                          vb[1 - par][:, 128 * c:128 * (c + 1)],
                                          atp[:], start=True, stop=False)
                            ins = te.matmul(psv[:, 128 * j:128 * (j + 1)],
                                            vb[par][:, 128 * c:128 * (c + 1)],
                                            atc[:], start=(not havep), stop=True)
                        self.inc(ins, 'pe', eng=te)
                    self.blk.tensor(fav)
                    tok_av = self.n['pe']
                    if grp == 3:
                        last_av[0] = tok_av

                    def fve(v, w=w, grp=grp, psv=psv, gpar=gpar, tok_av=tok_av):
                        v.wait_ge(self.S['pe'], tok_av)
                        ins = v.tensor_copy(
                            oTv[:, 4 * grp:4 * grp + 4, 128 * w:128 * (w + 1)],
                            psv[:].rearrange("p (j tt) -> p j tt", tt=128))
                        self.inc(ins, 'dv', eng=v)
                    self.blk.vector(fve)
                    evac_guard[gpar] = self.n['dv']
                if w + 1 < NT:
                    loadw(w + 1)
            tok_oT = self.n['dv']
            if self.debug:
                def fdbg(sp):
                    sp.wait_ge(self.S['dv'], tok_oT)
                    self.dma(sp, 'dbg', self.dbg_oT[:], oT[:])
                self.blk.scalar(fdbg)

            wob = [self.sb(f"p6_wo{i}", [128, ND * 512], BF16, es) for i in range(2)]
            wov = [w[:].rearrange("p (c o) -> p c o", o=512) for w in wob]
            xpc = [self.sb(f"p6_xp{i}", [128, 512], F32, es) for i in range(2)]
            xo = [self.sb(f"p6_xo{i}", [128, 512], F32, es) for i in range(2)]
            wo_tok = [None, None]
            wo_guard = [None, None]
            xo_guard = [None, None]
            xpc_guard = [None, None]
            psg6 = [None, None]

            def loadwo(os):
                par = os % 2

                def fg(g, os=os, par=par, gg=wo_guard[par]):
                    if gg is not None:
                        g.wait_ge(self.S['pe'], gg)
                    wo_tok[par] = self.dma(g, f'wo{par}', wov[par], self.woS[os])
                self.blk.gpsimd(fg)

            loadwo(0)
            for os in range(4):
                par = os % 2
                if os + 1 < 4:
                    loadwo(os + 1)
                for t in range(NT):
                    if 'p6' in SERIAL:
                        self.quiesce()
                    if 'p6pe' in SERIAL:
                        snap = dict(self.n)
                        def fpe(te, snap=snap):
                            for s_, v_ in snap.items():
                                if v_:
                                    te.wait_ge(self.S[s_], v_)
                        self.blk.tensor(fpe)
                    pmm = (self.psA, self.psB)[t % 2]
                    ppar = t % 2

                    def fsx(sq, os=os, t=t, ppar=ppar, g=xpc_guard[ppar]):
                        if g is not None:
                            sq.wait_ge(self.S['dv'], g)
                        self.dma(sq, f'xp{ppar}', xpc[ppar][:],
                                 x_in[128 * t:128 * (t + 1), 512 * os:512 * (os + 1)])
                    self.blk.sync(fsx)
                    tok_xp = self.n[f'xp{ppar}']

                    def fmm(te, os=os, t=t, pmm=pmm, ppar=ppar, par=par,
                            wt=wo_tok[par], g=psg6[ppar]):
                        te.wait_ge(self.S[f'wo{par}'], wt)
                        if os == 0 and t == 0:
                            te.wait_ge(self.S['dv'], tok_oT)
                        if g is not None:
                            te.wait_ge(self.S['dv'], g)
                        for c in range(ND):
                            ins = te.matmul(pmm[:], oTv[:, c, 128 * t:128 * (t + 1)],
                                            wov[par][:, c, :],
                                            start=(c == 0), stop=(c == ND - 1))
                        self.inc(ins, 'pe', eng=te)
                    self.blk.tensor(fmm)
                    tok_mm = self.n['pe']
                    if t == NT - 1:
                        wo_guard[par] = tok_mm

                    def fvu(v, pmm=pmm, ppar=ppar, tok_mm=tok_mm, tok_xp=tok_xp,
                            g=xo_guard[ppar]):
                        v.wait_ge(self.S['pe'], tok_mm)
                        v.wait_ge(self.S[f'xp{ppar}'], tok_xp)
                        if g is not None:
                            v.wait_ge(self.S[f'p{ppar}'], g)
                        ins = v.scalar_tensor_tensor(xo[ppar][:], pmm[:], 1.0,
                                                     xpc[ppar][:], ALU.mult, ALU.add)
                        self.inc(ins, 'dv', eng=v)
                    self.blk.vector(fvu)
                    tok_u = self.n['dv']
                    psg6[ppar] = tok_u
                    xpc_guard[ppar] = tok_u

                    def fso(sp, os=os, t=t, ppar=ppar, tok_u=tok_u):
                        sp.wait_ge(self.S['dv'], tok_u)
                        tok = self.dma(sp, f'p{ppar}',
                                       x_out[128 * t:128 * (t + 1),
                                             512 * os:512 * (os + 1)],
                                       xo[ppar][:])
                        xo_guard[ppar] = tok
                    self.blk.scalar(fso)
        self.quiesce()

    def build(self):
        self.p0()
        with ExitStack() as ffn_es:
            q1T = self.sb("q1T_a", [128, ND * self.T], BF16, ffn_es)
            self.p_h0(self.x0, q1T)
            self.p_m1(True, q1T)
        self.p_m2(self.x0, self.x1s, 'p')
        self.p_qkv(self.x1s)
        self.p_attn(self.x1s, self.x2s)
        with ExitStack() as ffn_es:
            q1T = self.sb("q1T_b", [128, ND * self.T], BF16, ffn_es)
            self.p_h0(self.x2s, q1T)
            self.p_m1(False, q1T)
        self.p_m2(self.x2s, self.outd, 'o')
        snap = dict(self.n)

        def fin(sq, snap=snap):
            for s, v in snap.items():
                if v:
                    sq.wait_ge(self.S[s], v)
        self.blk.sync(fin)

    def finish(self):
        self.es.close()
        return self.nc


def build_program(T=2048, debug=False):
    mk = MK(T, debug=debug)
    mk.build()
    return mk.finish()


def host_prep(x_full, w_attn, w_o, ln_w, ln_b, W1, b1, W2, b2):
    f = np.float32
    bf = ml_dtypes.bfloat16
    wa = w_attn.astype(f, copy=False)
    common = {
        "watS": np.ascontiguousarray(
            wa[:4096].reshape(32, 128, 16, 128).transpose(0, 3, 2, 1)),
        "wvS": np.ascontiguousarray(
            wa[4096:].reshape(4, 512, 16, 128).transpose(0, 3, 2, 1)),
        "woS": np.ascontiguousarray(
            w_o.astype(f, copy=False).reshape(4, 512, 16, 128).transpose(0, 3, 2, 1)),
        "w1": np.ascontiguousarray(W1.astype(f, copy=False)),
        "w2": np.ascontiguousarray(W2.astype(f, copy=False)),
        "lnw128": np.ascontiguousarray(np.broadcast_to(ln_w[None, :], (128, D)).astype(f)),
        "lnb128": np.ascontiguousarray(np.broadcast_to(ln_b[None, :], (128, D)).astype(f)),
        "b1v": np.ascontiguousarray(b1[None, :].astype(f)),
        "b2v": np.ascontiguousarray(b2[None, :].astype(f)),
        "identd": np.eye(128, dtype=bf),
        "ones2d": np.ones((2, 128), dtype=bf),
        "maskpd": np.tril(np.ones((128, 128), dtype=f)),
        "maskcd": np.triu(np.ones((128, 128), dtype=f)),
    }
    xs = [np.ascontiguousarray(x_full[i].astype(f, copy=False))
          for i in range(x_full.shape[0])]
    return common, xs


def kernel(L0_attn_x, L0_attn_w_attn, L0_attn_w_o, L0_ffn_ln_w, L0_ffn_ln_b,
           L0_ffn_W1, L0_ffn_b1, L0_ffn_W2, L0_ffn_b2):
    x = np.asarray(L0_attn_x)
    B, T, _ = x.shape
    common, xs = host_prep(x, np.asarray(L0_attn_w_attn), np.asarray(L0_attn_w_o),
                           np.asarray(L0_ffn_ln_w), np.asarray(L0_ffn_ln_b),
                           np.asarray(L0_ffn_W1), np.asarray(L0_ffn_b1),
                           np.asarray(L0_ffn_W2), np.asarray(L0_ffn_b2))
    nc = build_program(T)
    in_maps = [dict(common, x0=xs[i]) for i in range(B)]
    res = run_bass_kernel_spmd(nc, in_maps, core_ids=list(range(B)))
    return np.stack([np.asarray(r["out"]) for r in res.results], axis=0)


# version 07
# speedup vs baseline: 1.0899x; 1.0899x over previous
import numpy as np
import ml_dtypes
from contextlib import ExitStack
import os as _os
import concourse.bass as bass
import concourse.mybir as mybir
from concourse.bass_utils import run_bass_kernel_spmd

SERIAL = set((_os.environ.get("MK_SERIAL") or "").split(","))

F32 = mybir.dt.float32
BF16 = mybir.dt.bfloat16
I32 = mybir.dt.int32
ALU = mybir.AluOpType
AFT = mybir.ActivationFunctionType
AXX = mybir.AxisListType.X

D = 2048
FF = 8192
ND = D // 128
NF = FF // 128
NSL = FF // 512
MAGIC = float(1.5 * 2**17)
HIC = float(127.0 / 64.0)
EPS = 1e-5
EXPMASK = 0x7F800000


class MK:
    def __init__(self, T=2048, debug=False):
        self.T = T
        self.debug = debug
        self.NT = T // 128
        nc = bass.Bass("TRN2", target_bir_lowering=False, detect_race_conditions=False)
        self.nc = nc
        self.es = ExitStack()
        dr = lambda n, sh, dt, k: nc.dram_tensor(n, sh, dt, kind=k).ap()
        self.x0 = dr("x0", [T, D], F32, "ExternalInput")
        self.watS = dr("watS", [2 * ND, 128, ND, 128], F32, "ExternalInput")
        self.wvS = dr("wvS", [4, 128, ND, 512], F32, "ExternalInput")
        self.woS = dr("woS", [4, 128, ND, 512], F32, "ExternalInput")
        self.w1 = dr("w1", [FF, D], F32, "ExternalInput")
        self.w2 = dr("w2", [D, FF], F32, "ExternalInput")
        self.lnw128 = dr("lnw128", [128, D], F32, "ExternalInput")
        self.lnb128 = dr("lnb128", [128, D], F32, "ExternalInput")
        self.b1v = dr("b1v", [1, FF], F32, "ExternalInput")
        self.b2v = dr("b2v", [1, D], F32, "ExternalInput")
        self.identd = dr("identd", [128, 128], BF16, "ExternalInput")
        self.ones2d = dr("ones2d", [2, 128], BF16, "ExternalInput")
        self.maskpd = dr("maskpd", [128, 128], F32, "ExternalInput")
        self.maskcd = dr("maskcd", [128, 128], F32, "ExternalInput")
        self.outd = dr("out", [T, D], F32, "ExternalOutput")
        dk = "ExternalOutput" if debug else "Internal"
        self.x1s = dr("x1s", [T, D], F32, dk)
        self.x2s = dr("x2s", [T, D], F32, dk)
        self.qw1t_s = dr("qw1t_s", [NSL, 128, ND, 512], BF16, "Internal")
        self.qw2t_s = dr("qw2t_s", [8, 128, NF, 256], BF16, "Internal")
        self.q2t_s = dr("q2t_s", [self.NT, 128, NF, 128], BF16, "Internal")
        self.qt_s = dr("qt_s", [self.NT, 128, ND, 128], BF16, dk)
        self.kt_s = dr("kt_s", [self.NT, 128, ND, 128], BF16, dk)
        self.v_s = dr("v_s", [T, D], BF16, dk)
        self.dbg_oT = dr("dbg_oT", [128, ND * T], BF16, dk) if debug else None
        self.S = {}
        self.n = {}
        for s in ("pe", "dv", "ac", "gp"):
            self.sem(s)
        sb = self.sb
        self.ident = sb("ident", [128, 128], BF16)
        self.ones2 = sb("ones2", [2, 128], BF16)
        self.b1hl = sb("b1hl", [2, FF], BF16)
        self.b2hl = sb("b2hl", [2, D], BF16)
        self.maskp = sb("maskp", [128, 128], F32)
        self.maskc = sb("maskc", [128, 128], F32)
        self.lnsave = sb("lnsave", [128, 2 * self.NT], F32)
        self.st24 = [sb(f"st24_{i}", [128, 24], F32) for i in range(2)]
        self.mv = [sb(f"mv_{i}", [128, 2], F32) for i in range(2)]
        self.vp = [sb(f"vp_{i}", [128, 1], F32) for i in range(2)]
        self.rt = [sb(f"rt_{i}", [128, 1], F32) for i in range(2)]
        self.tn = [sb(f"tn_{i}", [128, 1], F32) for i in range(2)]
        self.mv2 = [sb(f"mv2_{i}", [128, 2], F32) for i in range(2)]
        self.vp2 = [sb(f"vp2_{i}", [128, 1], F32) for i in range(2)]
        self.rt2 = [sb(f"rt2_{i}", [128, 1], F32) for i in range(2)]
        self.rs2 = [sb(f"rs2_{i}", [128, 1], F32) for i in range(2)]
        self.qam = [sb(f"qam_{i}", [128, 128], F32) for i in range(2)]
        self.qse = [sb(f"qse_{i}", [128, 128], F32) for i in range(2)]
        self.qmm = [sb(f"qmm_{i}", [128, 128], F32) for i in range(2)]
        self.qhi = [sb(f"qhi_{i}", [128, 128], F32) for i in range(2)]
        ps = lambda n: self.es.enter_context(nc.psum_tensor(n, [128, 512], F32))
        self.psA = ps("psA"); self.psB = ps("psB")
        self.psT1 = ps("psT1"); self.psT2 = ps("psT2")
        self.psS1 = ps("psS1"); self.psS2 = ps("psS2")
        self.psV1 = ps("psV1"); self.psV2 = ps("psV2")
        self.blk = self.es.enter_context(nc.Block())

    def sem(self, name):
        if name not in self.S:
            self.S[name] = self.es.enter_context(self.nc.semaphore("sem_" + name))
            self.n[name] = 0
        return self.S[name]

    def tb(self, ps, j):
        return ps[:, 64 * j:64 * (j + 1)].bitcast(BF16)

    def tball(self, ps):
        return ps[:, 0:256].bitcast(BF16).rearrange("p (j tt) -> p j tt", tt=128)

    def sb(self, name, shape, dt, es=None):
        self._uc = getattr(self, '_uc', 0) + 1
        return (es or self.es).enter_context(
            self.nc.sbuf_tensor(f"{name}_u{self._uc}", shape, dt))

    def inc(self, instr, sem, k=1, eng=None):
        self.sem(sem)
        if eng is not None:
            instr = eng.drain()
        instr.then_inc(self.S[sem], k)
        self.n[sem] += k
        return self.n[sem]

    def dma(self, eng, sem, out, in_):
        self.sem(sem)
        ins = eng.dma_start(out=out, in_=in_)
        return self.inc(ins, sem, 16)

    def quiesce(self):
        snap = dict(self.n)
        for en in ("sync", "scalar", "vector", "tensor", "gpsimd"):
            def f(e, snap=snap):
                for s, v in snap.items():
                    if v:
                        e.wait_ge(self.S[s], v)
            getattr(self.blk, en)(f)

    def p0(self):
        nt = {}
        with ExitStack() as es:
            b1f = self.sb("p0_b1f", [1, FF], F32, es)
            b2f = self.sb("p0_b2f", [1, D], F32, es)
            lof = self.sb("p0_lof", [1, FF], F32, es)

            def fsync(sq):
                self.dma(sq, 'cst', self.ident[:], self.identd[:])
                self.dma(sq, 'cst', self.ones2[:], self.ones2d[:])
                self.dma(sq, 'cst', self.maskp[:], self.maskpd[:])
                self.dma(sq, 'cst', self.maskc[:], self.maskcd[:])
                self.dma(sq, 'cst', b1f[:], self.b1v[:])
                nt['b'] = self.dma(sq, 'cst', b2f[:], self.b2v[:])
            self.blk.sync(fsync)

            lof2 = self.sb("p0_lof2", [1, D], F32, es)

            def fv(v):
                v.wait_ge(self.S['cst'], nt['b'])
                v.tensor_copy(self.b1hl[0:1, :], b1f[:])
                v.drain()
                v.tensor_tensor(lof[:], b1f[:], self.b1hl[0:1, :], ALU.subtract)
                v.tensor_copy(self.b2hl[0:1, :], b2f[:])
                v.drain()
                ins = v.tensor_tensor(lof2[:], b2f[:], self.b2hl[0:1, :], ALU.subtract)
                v.drain()
                nt['v'] = self.inc(ins, 'dv', eng=v)
            self.blk.vector(fv)

            def fg(g):
                g.wait_ge(self.S['dv'], nt['v'])
                self.dma(g, 'bhl', self.b1hl[1:2, :], lof[:])
                self.dma(g, 'bhl', self.b2hl[1:2, :], lof2[:])
            self.blk.gpsimd(fg)
        self.quiesce()

    def emit_ln_stats(self, src_ap, par, ln0, rs_out, tok_src_sem=None, tok_src=None):
        st24 = self.st24[par]
        mv = self.mv[par] if ln0 else self.mv2[par]
        vp = self.vp[par] if ln0 else self.vp2[par]
        rt = self.rt[par] if ln0 else self.rt2[par]
        tn = self.tn[par]
        nt = {}

        def fv(v):
            if tok_src is not None:
                v.wait_ge(self.S[tok_src_sem], tok_src)
            for c in range(4):
                v.bn_stats(st24[:, 6 * c:6 * c + 6], src_ap[:, 512 * c:512 * (c + 1)])
            v.drain()
            v.bn_aggr(mv[:], st24[:])
            v.drain()
            ins = v.tensor_scalar(vp[:], mv[:, 1:2], EPS, None, ALU.add)
            nt['v'] = self.inc(ins, 'dv', eng=v)
        self.blk.vector(fv)

        def fs(s):
            s.wait_ge(self.S['dv'], nt['v'])
            ins = s.activation(rt[:], vp[:], AFT.Sqrt)
            nt['a'] = self.inc(ins, 'ac')
        self.blk.scalar(fs)

        def fv2(v):
            v.wait_ge(self.S['ac'], nt['a'])
            v.reciprocal(rs_out, rt[:])
            v.drain()
            v.tensor_tensor(tn[:], rs_out, rs_out, ALU.mult)
            v.drain()
            v.tensor_tensor(tn[:], tn[:], vp[:], ALU.mult)
            v.drain()
            v.tensor_scalar(tn[:], tn[:], -0.5, 1.5, ALU.mult, ALU.add)
            v.drain()
            ins = v.tensor_tensor(rs_out, rs_out, tn[:], ALU.mult)
            self.inc(ins, 'dv', eng=v)
        self.blk.vector(fv2)
        return mv

    def emit_quant(self, src, dst, tmp, n, par, split=False):
        split = False
        nb = n // 32
        am = self.qam[par][:, 0:nb]
        se = self.qse[par][:, 0:nb]
        mm = self.qmm[par][:, 0:nb]
        hi = self.qhi[par][:, 0:nb]
        bv = lambda ap: ap.rearrange("p (nb b) -> p nb b", b=32)
        mmb = mm.unsqueeze(2).broadcast_to([128, nb, 32])
        hib = hi.unsqueeze(2).broadcast_to([128, nb, 32])
        nt = {}

        def fv(v):
            v.drain()
            v.tensor_reduce(am, bv(src), AXX, ALU.max, apply_absolute_value=True)
            v.drain()
            v.tensor_scalar(se.bitcast(I32), am.bitcast(I32), EXPMASK, None,
                            ALU.bitwise_and)
            v.drain()
            v.tensor_scalar(mm, se, MAGIC, None, ALU.mult)
            ins = v.tensor_scalar(hi, se, HIC, None, ALU.mult)
            v.drain()
            nt['pre'] = self.inc(ins, 'dv', eng=v)
        self.blk.vector(fv)

        if split:
            def fg(g):
                g.wait_ge(self.S['dv'], nt['pre'])
                ins = g.tensor_tensor(bv(tmp), bv(src), mmb, ALU.add)
                nt['add'] = self.inc(ins, 'gp', eng=g)
            self.blk.gpsimd(fg)

            def fv2(v):
                v.wait_ge(self.S['gp'], nt['add'])
                ins = v.tensor_tensor(bv(tmp), bv(tmp), mmb, ALU.subtract)
                nt['sub'] = self.inc(ins, 'dv', eng=v)
            self.blk.vector(fv2)

            def fg2(g):
                g.wait_ge(self.S['dv'], nt['sub'])
                ins = g.tensor_tensor(bv(dst), bv(tmp), hib, ALU.min)
                nt['tok'] = self.inc(ins, 'gp', eng=g)
            self.blk.gpsimd(fg2)
            return ('gp', nt['tok'])
        else:
            def fv2(v):
                v.tensor_tensor(bv(tmp), bv(src), mmb, ALU.add)
                v.tensor_tensor(bv(tmp), bv(tmp), mmb, ALU.subtract)
                ins = v.tensor_tensor(bv(dst), bv(tmp), hib, ALU.min)
                nt['tok'] = self.inc(ins, 'dv', eng=v)
            self.blk.vector(fv2)
            return ('dv', nt['tok'])

    def p_h0(self, x_in, q1T):
        NT = self.NT
        q1Tv = q1T[:].rearrange("p (c tt) -> p c tt", tt=self.T)
        with ExitStack() as es:
            lnw = self.sb("p1_lnw", [128, D], F32, es)
            lnb = self.sb("p1_lnb", [128, D], F32, es)
            xa = [self.sb(f"p1_xa{i}", [128, D], F32, es) for i in range(2)]
            xbt = [self.sb(f"p1_xb{i}", [128, D], BF16, es) for i in range(2)]
            ut = self.sb("p1_u", [128, D], F32, es)
            h0 = [self.sb(f"p1_h0{i}", [128, D], BF16, es) for i in range(2)]
            tmp = self.sb("p1_tmp", [128, D], F32, es)
            q1 = [self.sb(f"p1_q1{i}", [128, D], BF16, es) for i in range(2)]

            def fsync(sq):
                self.dma(sq, 'cst', lnw[:], self.lnw128[:])
                self.dma(sq, 'cst', lnb[:], self.lnb128[:])
            self.blk.sync(fsync)
            tok_cst = self.n['cst']

            guard_xa = [None, None]
            guard_xb = [None, None]
            guard_q1 = [None, None]
            evac_tok = [None, None]
            for t in range(NT):
                if 'p1' in SERIAL:
                    self.quiesce()
                par = t % 2

                def fsync2(sq, t=t, par=par):
                    if guard_xa[par] is not None:
                        sq.wait_ge(self.S['dv'], guard_xa[par])
                    self.dma(sq, f'xa{par}', xa[par][:],
                             x_in[128 * t:128 * (t + 1), :])
                self.blk.sync(fsync2)
                tok_x = self.n[f'xa{par}']

                self.emit_ln_stats(xa[par][:], par, True,
                                   self.lnsave[:, 2 * t:2 * t + 1],
                                   tok_src_sem=f'xa{par}', tok_src=tok_x)
                mv = self.mv[par]

                def fv(v, t=t, mv=mv):
                    v.drain()
                    ins = v.tensor_scalar(self.lnsave[:, 2 * t + 1:2 * t + 2],
                                          mv[:, 0:1], self.lnsave[:, 2 * t:2 * t + 1],
                                          -1.0, ALU.mult, ALU.mult)
                    v.drain()
                    self.inc(ins, 'dv', eng=v)
                self.blk.vector(fv)
                tok_nm = self.n['dv']

                def fs(s, t=t, par=par, tok_nm=tok_nm, gxb=guard_xb[par]):
                    s.wait_ge(self.S['dv'], tok_nm)
                    if gxb is not None:
                        s.wait_ge(self.S['dv'], gxb)
                    ins = s.activation(xbt[par][:], xa[par][:], AFT.Identity,
                                       bias=self.lnsave[:, 2 * t + 1:2 * t + 2],
                                       scale=self.lnsave[:, 2 * t:2 * t + 1])
                    self.inc(ins, 'ac')
                self.blk.scalar(fs)
                tok_xb = self.n['ac']

                self.emit_ln_stats(xbt[par][:], par, False, self.rs2[par][:],
                                   tok_src_sem='ac', tok_src=tok_xb)
                mv2 = self.mv2[par]

                def fv2(v, par=par, mv2=mv2):
                    v.drain()
                    v.wait_ge(self.S['cst'], tok_cst)
                    v.tensor_scalar(ut[:], xbt[par][:], mv2[:, 0:1],
                                    self.rs2[par][:], ALU.subtract, ALU.mult)
                    v.tensor_tensor(ut[:], ut[:], lnw[:], ALU.mult)
                    ins = v.tensor_tensor(h0[par][:], ut[:], lnb[:], ALU.add)
                    self.inc(ins, 'dv', eng=v)
                self.blk.vector(fv2)
                guard_xa[par] = self.n['dv']
                guard_xb[par] = self.n['dv']

                def fvq(v, par=par, gq=guard_q1[par]):
                    if gq is not None:
                        v.wait_ge(self.S['pe'], gq)
                self.blk.vector(fvq)
                qsem, qtok = self.emit_quant(h0[par][:], q1[par][:], tmp[:], D, par)

                for grp in range(4):
                    pst = (self.psT1, self.psT2)[grp % 2]
                    gpar = grp % 2

                    def ft(te, par=par, grp=grp, pst=pst, qsem=qsem, qtok=qtok,
                           ev=evac_tok[grp % 2]):
                        te.wait_ge(self.S[qsem], qtok)
                        if ev is not None:
                            te.wait_ge(self.S['dv'], ev)
                        for j in range(4):
                            c = 4 * grp + j
                            ins = te.transpose(self.tb(pst, j),
                                               q1[par][:, 128 * c:128 * (c + 1)],
                                               self.ident[:])
                        self.inc(ins, 'pe')
                    self.blk.tensor(ft)
                    tok_pe = self.n['pe']
                    if grp == 3:
                        guard_q1[par] = tok_pe

                    def fse(v, t=t, grp=grp, pst=pst, tok_pe=tok_pe):
                        v.wait_ge(self.S['pe'], tok_pe)
                        ins = v.tensor_copy(
                            q1Tv[:, 4 * grp:4 * grp + 4, 128 * t:128 * (t + 1)],
                            self.tball(pst))
                        self.inc(ins, 'dv', eng=v)
                    self.blk.vector(fse)
                    evac_tok[gpar] = self.n['dv']
        self.quiesce()

    def p_m1(self, first, q1T):
        NT = self.NT
        q1Tv = q1T[:].rearrange("p (c tt) -> p c tt", tt=self.T)
        with ExitStack() as es:
            wn = [self.sb(f"p2_wn{i}", [128, D], F32, es) for i in range(2)]
            qwn = [self.sb(f"p2_qwn{i}", [128, D], BF16, es) for i in range(2)]
            wtmp = [self.sb(f"p2_wtmp{i}", [128, D], F32, es) for i in range(2)]
            wbuf = [self.sb(f"p2_wbuf{i}", [128, ND * 512], BF16, es) for i in range(2)]
            w2st = [self.sb(f"p2_w2st{i}", [128, 4 * 128], BF16, es) for i in range(2)]
            gsb = [self.sb(f"p2_g{i}", [128, 512], BF16, es) for i in range(2)]
            q2p = [self.sb(f"p2_q2{i}", [128, 512], BF16, es) for i in range(2)]
            t2f = [self.sb(f"p2_t2f{i}", [128, 512], F32, es) for i in range(2)]
            q2st = [self.sb(f"p2_q2st{i}", [128, 4 * 128], BF16, es) for i in range(2)]
            wbufv = [w[:].rearrange("p (c o) -> p c o", o=512) for w in wbuf]

            wq_done = [None, None]
            mm_done = [None, None]
            st_done = [None, None]
            w2st_done = [None, None]
            g_guard = [None, None]
            q2_guard = [None, None]
            wn_guard = [None, None]
            qwn_guard = [None, None]
            psmm_guard = [None, None]
            pst2_guard = [None]
            pst1_guard = [None]
            wcount = [0]

            def w1_rt(sl, rt):
                bpar = sl % 2
                wpar = wcount[0] % 2
                wcount[0] += 1

                def fsync(sq, sl=sl, rt=rt, wpar=wpar, g=wn_guard[wpar]):
                    if g is not None:
                        sq.wait_ge(self.S[g[0]], g[1])
                    r0 = 512 * sl + 128 * rt
                    self.dma(sq, f'wn{wpar}', wn[wpar][:], self.w1[r0:r0 + 128, :])
                self.blk.sync(fsync)
                tok_ld = self.n[f'wn{wpar}']

                def fv(v, tok_ld=tok_ld, wpar=wpar, g=qwn_guard[wpar]):
                    v.wait_ge(self.S[f'wn{wpar}'], tok_ld)
                    if g is not None:
                        v.wait_ge(self.S['pe'], g)
                self.blk.vector(fv)
                qsem, qtok = self.emit_quant(wn[wpar][:], qwn[wpar][:],
                                             wtmp[wpar][:], D, wpar, split=False)
                wn_guard[wpar] = (qsem, qtok)
                for grp in range(4):
                    def ft(te, grp=grp, wpar=wpar, qsem=qsem, qtok=qtok,
                           g=pst2_guard[0]):
                        te.wait_ge(self.S[qsem], qtok)
                        if g is not None:
                            te.wait_ge(self.S['dv'], g)
                        for j in range(4):
                            c = 4 * grp + j
                            ins = te.transpose(self.tb(self.psT2, j),
                                               qwn[wpar][:, 128 * c:128 * (c + 1)],
                                               self.ident[:])
                        self.inc(ins, 'pe')
                    self.blk.tensor(ft)
                    tok_pe = self.n['pe']
                    if grp == 3:
                        qwn_guard[wpar] = tok_pe

                    def fse(v, rt=rt, grp=grp, bpar=bpar, tok_pe=tok_pe,
                            mmd=mm_done[bpar]):
                        v.wait_ge(self.S['pe'], tok_pe)
                        if mmd is not None and rt == 0 and grp == 0:
                            v.wait_ge(self.S['pe'], mmd)
                        ins = v.tensor_copy(
                            wbufv[bpar][:, 4 * grp:4 * grp + 4,
                                        128 * rt:128 * (rt + 1)],
                            self.tball(self.psT2))
                        self.inc(ins, 'dv', eng=v)
                    self.blk.vector(fse)
                    pst2_guard[0] = self.n['dv']

            def w1_store(sl):
                bpar = sl % 2
                tok = self.n['dv']

                def fst(sp, sl=sl, bpar=bpar, tok=tok):
                    sp.wait_ge(self.S['dv'], tok)
                    self.dma(sp, f'qws{bpar}', self.qw1t_s[sl], wbufv[bpar])
                self.blk.scalar(fst)

            def w1_load(sl):
                bpar = sl % 2

                def fsync(sq, sl=sl, bpar=bpar, mmd=mm_done[bpar]):
                    if mmd is not None:
                        sq.wait_ge(self.S['pe'], mmd)
                    tok = self.dma(sq, f'wb{bpar}', wbufv[bpar], self.qw1t_s[sl])
                    wq_done[bpar] = (f'wb{bpar}', tok)
                self.blk.sync(fsync)

            def w2_qt(qi):
                rti, kq = qi // 4, qi % 4
                wpar = wcount[0] % 2
                wcount[0] += 1

                def fsync(sq, rti=rti, kq=kq, wpar=wpar, g=wn_guard[wpar]):
                    if g is not None:
                        sq.wait_ge(self.S[g[0]], g[1])
                    self.dma(sq, f'wn{wpar}', wn[wpar][:],
                             self.w2[128 * rti:128 * (rti + 1),
                                     2048 * kq:2048 * (kq + 1)])
                self.blk.sync(fsync)
                tok_ld = self.n[f'wn{wpar}']

                def fv(v, tok_ld=tok_ld, wpar=wpar, g=qwn_guard[wpar]):
                    v.wait_ge(self.S[f'wn{wpar}'], tok_ld)
                    if g is not None:
                        v.wait_ge(self.S['pe'], g)
                self.blk.vector(fv)
                qsem, qtok = self.emit_quant(wn[wpar][:], qwn[wpar][:],
                                             wtmp[wpar][:], D, wpar, split=False)
                wn_guard[wpar] = (qsem, qtok)
                for grp in range(4):
                    spar = grp % 2

                    def ft(te, grp=grp, wpar=wpar, qsem=qsem, qtok=qtok,
                           g=pst2_guard[0]):
                        te.wait_ge(self.S[qsem], qtok)
                        if g is not None:
                            te.wait_ge(self.S['dv'], g)
                        for j in range(4):
                            c = 4 * grp + j
                            ins = te.transpose(self.tb(self.psT2, j),
                                               qwn[wpar][:, 128 * c:128 * (c + 1)],
                                               self.ident[:])
                        self.inc(ins, 'pe')
                    self.blk.tensor(ft)
                    tok_pe = self.n['pe']
                    if grp == 3:
                        qwn_guard[wpar] = tok_pe

                    def fse(v, grp=grp, spar=spar, tok_pe=tok_pe,
                            g=w2st_done[spar]):
                        v.wait_ge(self.S['pe'], tok_pe)
                        if g is not None:
                            v.wait_ge(self.S[f'w2s{spar}'], g)
                        ins = v.tensor_copy(
                            w2st[spar][:].rearrange("p (j d) -> p j d", d=128),
                            self.tball(self.psT2))
                        self.inc(ins, 'dv', eng=v)
                    self.blk.vector(fse)
                    pst2_guard[0] = self.n['dv']
                    tok_ev = self.n['dv']

                    def fst(sp, rti=rti, kq=kq, grp=grp, spar=spar, tok_ev=tok_ev):
                        sp.wait_ge(self.S['dv'], tok_ev)
                        H = rti // 2
                        c0 = 16 * kq + 4 * grp
                        dof = 128 * (rti % 2)
                        tok = self.dma(
                            sp, f'w2s{spar}',
                            self.qw2t_s[H, :, c0:c0 + 4, dof:dof + 128],
                            w2st[spar][:].rearrange("p (j d) -> p j d", d=128))
                        w2st_done[spar] = tok
                    self.blk.scalar(fst)

            if first:
                for rt in range(4):
                    w1_rt(0, rt)
                wq_done[0] = ('dv', self.n['dv'])
                w1_store(0)
            else:
                w1_load(0)

            nw2 = [0]

            def emit_witem(it):
                kind, a = it
                if kind == 'w1':
                    w1_rt(a[0], a[1])
                    if a[1] == 3:
                        wq_done[a[0] % 2] = ('dv', self.n['dv'])
                        w1_store(a[0])
                else:
                    w2_qt(a)

            for sl in range(NSL):
                bpar = sl % 2
                witems = []
                if first:
                    if sl + 1 < NSL:
                        witems += [('w1', (sl + 1, rt)) for rt in range(4)]
                    nav = min(4 if sl + 1 < NSL else 64 - nw2[0], 64 - nw2[0])
                    witems += [('w2', nw2[0] + i) for i in range(nav)]
                    nw2[0] += nav
                elif sl + 1 < NSL:
                    pass
                done = 0
                for t in range(NT):
                    if 'p2' in SERIAL:
                        self.quiesce()
                    want = ((t + 1) * len(witems) + NT - 1) // NT
                    while done < min(want, len(witems)):
                        emit_witem(witems[done])
                        done += 1
                    if (not first) and sl + 1 < NSL and t == 0:
                        w1_load(sl + 1)
                    wsem, wtok = wq_done[bpar]
                    pmm = (self.psA, self.psB)[t % 2]
                    ppar = t % 2

                    def fmm(te, sl=sl, t=t, pmm=pmm, ppar=ppar, wsem=wsem,
                            wtok=wtok, g=psmm_guard[ppar]):
                        te.wait_ge(self.S[wsem], wtok)
                        if g is not None:
                            te.wait_ge(self.S['ac'], g)
                        for c in range(ND):
                            te.matmul(pmm[:], q1Tv[:, c, 128 * t:128 * (t + 1)],
                                      wbufv[sl % 2][:, c, :],
                                      start=(c == 0), stop=False)
                        ins = te.matmul(pmm[:], self.ones2[:],
                                        self.b1hl[:, 512 * sl:512 * (sl + 1)],
                                        start=False, stop=True)
                        self.inc(ins, 'pe')
                    self.blk.tensor(fmm)
                    tok_mm = self.n['pe']
                    if t == NT - 1:
                        mm_done[bpar] = tok_mm

                    def fgelu(s, pmm=pmm, ppar=ppar, tok_mm=tok_mm,
                              g=g_guard[ppar]):
                        s.wait_ge(self.S['pe'], tok_mm)
                        if g is not None:
                            s.wait_ge(self.S['dv'], g)
                        ins = s.activation(gsb[ppar][:], pmm[:], AFT.Gelu_apprx_tanh)
                        self.inc(ins, 'ac')
                    self.blk.scalar(fgelu)
                    tok_g = self.n['ac']
                    psmm_guard[ppar] = tok_g

                    def fvw(v, ppar=ppar, tok_g=tok_g, g=q2_guard[ppar]):
                        v.wait_ge(self.S['ac'], tok_g)
                        if g is not None:
                            v.wait_ge(self.S['pe'], g)
                    self.blk.vector(fvw)
                    qsem, qtok = self.emit_quant(gsb[ppar][:], q2p[ppar][:],
                                                 t2f[ppar][:], 512, ppar)
                    g_guard[ppar] = self.n['dv']

                    def ftq(te, ppar=ppar, qsem=qsem, qtok=qtok, g=pst1_guard[0]):
                        te.wait_ge(self.S[qsem], qtok)
                        if g is not None:
                            te.wait_ge(self.S['dv'], g)
                        for j in range(4):
                            ins = te.transpose(self.tb(self.psT1, j),
                                               q2p[ppar][:, 128 * j:128 * (j + 1)],
                                               self.ident[:])
                        self.inc(ins, 'pe')
                    self.blk.tensor(ftq)
                    tok_tq = self.n['pe']
                    q2_guard[ppar] = tok_tq

                    def fve(v, ppar=ppar, tok_tq=tok_tq, g=st_done[ppar]):
                        v.wait_ge(self.S['pe'], tok_tq)
                        if g is not None:
                            v.wait_ge(self.S[f'q2s{ppar}'], g)
                        ins = v.tensor_copy(
                            q2st[ppar][:].rearrange("p (j tt) -> p j tt", tt=128),
                            self.tball(self.psT1))
                        self.inc(ins, 'dv', eng=v)
                    self.blk.vector(fve)
                    pst1_guard[0] = self.n['dv']
                    tok_ev = self.n['dv']

                    def fst(sp, sl=sl, t=t, ppar=ppar, tok_ev=tok_ev):
                        sp.wait_ge(self.S['dv'], tok_ev)
                        tok = self.dma(sp, f'q2s{ppar}',
                                       self.q2t_s[t, :, 4 * sl:4 * (sl + 1), :],
                                       q2st[ppar][:].rearrange(
                                           "p (j tt) -> p j tt", tt=128))
                        st_done[ppar] = tok
                    self.blk.scalar(fst)
                while done < len(witems):
                    emit_witem(witems[done])
                    done += 1
        self.quiesce()

    def p_m2(self, x_in, x_out, out_sem):
        NT = self.NT
        with ExitStack() as es:
            ring = [self.sb(f"p3_ring{i}", [128, NF * 256], BF16, es) for i in range(3)]
            ringv = [r[:].rearrange("p (c d) -> p c d", d=256) for r in ring]
            q2tb = [self.sb(f"p3_q2t{i}", [128, NF * 128], BF16, es) for i in range(2)]
            q2tv = [r[:].rearrange("p (c tt) -> p c tt", tt=128) for r in q2tb]
            xpc = [self.sb(f"p3_xp{i}", [128, 512], F32, es) for i in range(2)]
            xbp = [self.sb(f"p3_xbp{i}", [128, 512], BF16, es) for i in range(2)]
            h2p = [self.sb(f"p3_h2p{i}", [128, 512], BF16, es) for i in range(2)]
            up = [self.sb(f"p3_up{i}", [128, 512], BF16, es) for i in range(2)]
            xo = [self.sb(f"p3_xo{i}", [128, 512], F32, es) for i in range(2)]

            ring_ready = [None] * 3
            ring_guard = [None] * 3
            q2_ready = [None, None]
            q2_guard = [None, None]
            psmm_guard = [None] * 4
            xo_guard = [None, None]
            xpc_guard = [None, None]
            xbp_guard = [None, None]
            h2_guard = [None, None]

            def build_half(H):
                slot = H % 3

                def fsync(sq, H=H, slot=slot, g=ring_guard[slot]):
                    if g is not None:
                        sq.wait_ge(self.S['pe'], g)
                    tok = self.dma(sq, f'rg{slot}', ringv[slot], self.qw2t_s[H])
                    ring_ready[slot] = (f'rg{slot}', tok)
                self.blk.sync(fsync)

            def load_q2t(t):
                par = t % 2

                def fsync(sq, t=t, par=par, g=q2_guard[par]):
                    if g is not None:
                        sq.wait_ge(self.S['pe'], g)
                    tok = self.dma(sq, f'q2l{par}', q2tv[par], self.q2t_s[t])
                    q2_ready[par] = tok
                self.blk.sync(fsync)

            build_half(0)
            build_half(1)
            load_q2t(0)
            for sg in range(4):
                lo, hi = (2 * sg) % 3, (2 * sg + 1) % 3
                for t in range(NT):
                    if 'p3' in SERIAL:
                        self.quiesce()
                    if t == 0 and 2 * sg + 2 < 8:
                        build_half(2 * sg + 2)
                    if t + 1 < NT:
                        load_q2t(t + 1)
                    elif sg + 1 < 4:
                        load_q2t(0)
                    par = t % 2
                    pa = (self.psA, self.psS1)[par]
                    pb = (self.psB, self.psS2)[par]
                    ia, ib = (0, 1) if par == 0 else (2, 3)
                    rlo, rhi = ring_ready[lo], ring_ready[hi]

                    def fmm(te, sg=sg, t=t, par=par, pa=pa, pb=pb, lo=lo, hi=hi,
                            rlo=rlo, rhi=rhi, q2r=q2_ready[par],
                            ga=psmm_guard[ia], gb=psmm_guard[ib]):
                        te.wait_ge(self.S[rlo[0]], rlo[1])
                        te.wait_ge(self.S[rhi[0]], rhi[1])
                        te.wait_ge(self.S[f'q2l{par}'], q2r)
                        if ga is not None:
                            te.wait_ge(self.S['dv'], ga)
                        if gb is not None:
                            te.wait_ge(self.S['dv'], gb)
                        for c in range(NF):
                            te.matmul(pa[:, 0:256], q2tv[par][:, c, :],
                                      ringv[lo][:, c, :], start=(c == 0), stop=False)
                            te.matmul(pb[:, 0:256], q2tv[par][:, c, :],
                                      ringv[hi][:, c, :], start=(c == 0), stop=False)
                        te.matmul(pa[:, 0:256], self.ones2[:],
                                  self.b2hl[:, 512 * sg:512 * sg + 256],
                                  start=False, stop=True)
                        ins = te.matmul(pb[:, 0:256], self.ones2[:],
                                        self.b2hl[:, 512 * sg + 256:512 * (sg + 1)],
                                        start=False, stop=True)
                        self.inc(ins, 'pe')
                    self.blk.tensor(fmm)
                    tok_mm = self.n['pe']
                    q2_guard[par] = tok_mm
                    if t == NT - 1:
                        ring_guard[lo] = tok_mm
                        ring_guard[hi] = tok_mm

                    def fsx(sq, sg=sg, t=t, par=par, g=xpc_guard[par]):
                        if g is not None:
                            sq.wait_ge(self.S['dv'], g)
                        self.dma(sq, f'xp{par}', xpc[par][:],
                                 x_in[128 * t:128 * (t + 1), 512 * sg:512 * (sg + 1)])
                    self.blk.sync(fsx)
                    tok_xp = self.n[f'xp{par}']

                    def fh2(v, par=par, pa=pa, pb=pb, tok_mm=tok_mm,
                            gh=h2_guard[par]):
                        v.wait_ge(self.S['pe'], tok_mm)
                        if gh is not None:
                            v.wait_ge(self.S['dv'], gh)
                        v.tensor_copy(h2p[par][:, 0:256], pa[:, 0:256])
                        ins = v.tensor_copy(h2p[par][:, 256:512], pb[:, 0:256])
                        self.inc(ins, 'dv', eng=v)
                    self.blk.vector(fh2)
                    tok_h2d = self.n['dv']

                    def fac(s, t=t, par=par, tok_xp=tok_xp, gx=xbp_guard[par]):
                        s.wait_ge(self.S[f'xp{par}'], tok_xp)
                        if gx is not None:
                            s.wait_ge(self.S['dv'], gx)
                        ins = s.activation(xbp[par][:], xpc[par][:], AFT.Identity,
                                           bias=self.lnsave[:, 2 * t + 1:2 * t + 2],
                                           scale=self.lnsave[:, 2 * t:2 * t + 1])
                        self.inc(ins, 'ac')
                    self.blk.scalar(fac)
                    tok_h2 = self.n['ac']

                    def fvu(v, par=par, tok_h2=tok_h2, tok_h2d=tok_h2d,
                            g=xo_guard[par]):
                        v.wait_ge(self.S['ac'], tok_h2)
                        v.tensor_tensor(up[par][:], xbp[par][:], h2p[par][:], ALU.add)
                        if g is not None:
                            v.wait_ge(self.S[f'{out_sem}{par}'], g)
                        ins = v.scalar_tensor_tensor(xo[par][:], up[par][:], 0.5,
                                                     xpc[par][:], ALU.mult, ALU.add)
                        self.inc(ins, 'dv', eng=v)
                    self.blk.vector(fvu)
                    tok_u = self.n['dv']
                    psmm_guard[ia] = tok_u
                    psmm_guard[ib] = tok_u
                    xpc_guard[par] = tok_u
                    xbp_guard[par] = tok_u
                    h2_guard[par] = tok_u

                    def fso(sp, sg=sg, t=t, par=par, tok_u=tok_u):
                        sp.wait_ge(self.S['dv'], tok_u)
                        tok = self.dma(sp, f'{out_sem}{par}',
                                       x_out[128 * t:128 * (t + 1),
                                             512 * sg:512 * (sg + 1)],
                                       xo[par][:])
                        xo_guard[par] = tok
                    self.blk.scalar(fso)
                if 2 * sg + 3 < 8:
                    build_half(2 * sg + 3)
        self.quiesce()

    def p_qkv(self, x_in):
        NT, T = self.NT, self.T
        with ExitStack() as es:
            aT = self.sb("p4_aT", [128, ND * T], BF16, es)
            aTv = aT[:].rearrange("p (c tt) -> p c tt", tt=T)
            xa = [self.sb(f"p4_xa{i}", [128, D], F32, es) for i in range(2)]
            at = [self.sb(f"p4_a{i}", [128, D], BF16, es) for i in range(2)]
            wqb = [self.sb(f"p4_wq{i}", [128, ND * 128], BF16, es) for i in range(2)]
            wvb = [self.sb(f"p4_wv{i}", [128, ND * 512], BF16, es) for i in range(2)]
            qkst = [self.sb(f"p4_qkst{i}", [128, 512], BF16, es) for i in range(2)]
            vst = [self.sb(f"p4_vst{i}", [128, 512], BF16, es) for i in range(2)]
            wqv = [w[:].rearrange("p (c j) -> p c j", j=128) for w in wqb]
            wvv = [w[:].rearrange("p (c o) -> p c o", o=512) for w in wvb]

            guard_xa = [None, None]
            guard_a = [None, None]
            evac_tok = [None, None]
            for t in range(NT):
                if 'p4' in SERIAL:
                    self.quiesce()
                par = t % 2

                def fsync(sq, t=t, par=par, g=guard_xa[par]):
                    if g is not None:
                        sq.wait_ge(self.S['ac'], g)
                    self.dma(sq, f'xa{par}', xa[par][:],
                             x_in[128 * t:128 * (t + 1), :])
                self.blk.sync(fsync)
                tok_x = self.n[f'xa{par}']
                self.emit_ln_stats(xa[par][:], par, True,
                                   self.lnsave[:, 2 * t:2 * t + 1],
                                   tok_src_sem=f'xa{par}', tok_src=tok_x)
                mv = self.mv[par]

                def fv(v, t=t, mv=mv):
                    v.drain()
                    ins = v.tensor_scalar(self.lnsave[:, 2 * t + 1:2 * t + 2],
                                          mv[:, 0:1], self.lnsave[:, 2 * t:2 * t + 1],
                                          -1.0, ALU.mult, ALU.mult)
                    v.drain()
                    self.inc(ins, 'dv', eng=v)
                self.blk.vector(fv)
                tok_nm = self.n['dv']

                def fs(s, t=t, par=par, tok_nm=tok_nm, g=guard_a[par]):
                    s.wait_ge(self.S['dv'], tok_nm)
                    if g is not None:
                        s.wait_ge(self.S['pe'], g)
                    ins = s.activation(at[par][:], xa[par][:], AFT.Identity,
                                       bias=self.lnsave[:, 2 * t + 1:2 * t + 2],
                                       scale=self.lnsave[:, 2 * t:2 * t + 1])
                    self.inc(ins, 'ac')
                self.blk.scalar(fs)
                guard_xa[par] = self.n['ac']
                tok_a = self.n['ac']

                for grp in range(4):
                    pst = (self.psT1, self.psT2)[grp % 2]
                    gpar = grp % 2

                    def ft(te, par=par, grp=grp, pst=pst, tok_a=tok_a,
                           ev=evac_tok[grp % 2]):
                        te.wait_ge(self.S['ac'], tok_a)
                        if ev is not None:
                            te.wait_ge(self.S['dv'], ev)
                        for j in range(4):
                            c = 4 * grp + j
                            ins = te.transpose(self.tb(pst, j),
                                               at[par][:, 128 * c:128 * (c + 1)],
                                               self.ident[:])
                        self.inc(ins, 'pe')
                    self.blk.tensor(ft)
                    tok_pe = self.n['pe']
                    if grp == 3:
                        guard_a[par] = tok_pe

                    def fve(v, t=t, grp=grp, pst=pst, tok_pe=tok_pe):
                        v.wait_ge(self.S['pe'], tok_pe)
                        ins = v.tensor_copy(
                            aTv[:, 4 * grp:4 * grp + 4, 128 * t:128 * (t + 1)],
                            self.tball(pst))
                        self.inc(ins, 'dv', eng=v)
                    self.blk.vector(fve)
                    evac_tok[gpar] = self.n['dv']
            tok_aT = self.n['dv']

            wq_tok = [None, None]
            wq_guard = [None, None]
            st_guard = [None, None]
            psg = [None, None]

            def loadwq(oc):
                par = oc % 2

                def fg(g, oc=oc, par=par, gg=wq_guard[par]):
                    if gg is not None:
                        g.wait_ge(self.S['pe'], gg)
                    wq_tok[par] = self.dma(g, f'wq{par}', wqv[par], self.watS[oc])
                self.blk.gpsimd(fg)

            loadwq(0)
            for oc in range(2 * ND):
                if 'p4' in SERIAL:
                    self.quiesce()
                par = oc % 2
                if oc + 1 < 2 * ND:
                    loadwq(oc + 1)
                isq = oc < ND
                for ts4 in range(T // 512):
                    pmm = (self.psA, self.psB)[ts4 % 2]
                    ppar = ts4 % 2

                    def fmm(te, oc=oc, ts4=ts4, pmm=pmm, ppar=ppar, par=par,
                            wt=wq_tok[par], g=psg[ppar]):
                        te.wait_ge(self.S[f'wq{par}'], wt)
                        if oc == 0 and ts4 == 0:
                            te.wait_ge(self.S['dv'], tok_aT)
                        if g is not None:
                            te.wait_ge(self.S['dv'], g)
                        for c in range(ND):
                            ins = te.matmul(pmm[:], wqv[par][:, c, :],
                                            aTv[:, c, 512 * ts4:512 * (ts4 + 1)],
                                            start=(c == 0), stop=(c == ND - 1))
                        self.inc(ins, 'pe')
                    self.blk.tensor(fmm)
                    tok_mm = self.n['pe']
                    if ts4 == T // 512 - 1:
                        wq_guard[par] = tok_mm

                    def fse(v, pmm=pmm, ppar=ppar, isq=isq, tok_mm=tok_mm,
                            g=st_guard[ppar]):
                        v.wait_ge(self.S['pe'], tok_mm)
                        if g is not None:
                            v.wait_ge(self.S[f'qk{ppar}'], g)
                        if isq:
                            ins = v.tensor_scalar(qkst[ppar][:], pmm[:],
                                                  1.0 / 128.0, None, ALU.mult)
                        else:
                            ins = v.tensor_copy(qkst[ppar][:], pmm[:])
                        self.inc(ins, 'dv', eng=v)
                    self.blk.vector(fse)
                    tok_ev = self.n['dv']
                    psg[ppar] = tok_ev

                    def fst(sp, oc=oc, ts4=ts4, ppar=ppar, isq=isq, tok_ev=tok_ev):
                        sp.wait_ge(self.S['dv'], tok_ev)
                        dst = self.qt_s if isq else self.kt_s
                        occ = oc if isq else oc - ND
                        tok = self.dma(sp, f'qk{ppar}',
                                       dst[4 * ts4:4 * ts4 + 4, :, occ, :].rearrange(
                                           "tb p j -> p tb j"),
                                       qkst[ppar][:].rearrange(
                                           "p (tb j) -> p tb j", j=128))
                        st_guard[ppar] = tok
                    self.blk.scalar(fst)

            wv_tok = [None, None]
            wv_guard = [None, None]
            vst_guard = [None, None]
            psgv = [self.n['dv'], self.n['dv']]

            def loadwv(os):
                par = os % 2

                def fg(g, os=os, par=par, gg=wv_guard[par]):
                    if gg is not None:
                        g.wait_ge(self.S['pe'], gg)
                    wv_tok[par] = self.dma(g, f'wv{par}', wvv[par], self.wvS[os])
                self.blk.gpsimd(fg)

            loadwv(0)
            for os in range(4):
                if 'p4' in SERIAL:
                    self.quiesce()
                par = os % 2
                if os + 1 < 4:
                    loadwv(os + 1)
                for t in range(NT):
                    pmm = (self.psA, self.psB)[t % 2]
                    ppar = t % 2

                    def fmm(te, os=os, t=t, pmm=pmm, ppar=ppar, par=par,
                            wt=wv_tok[par], g=psgv[ppar]):
                        te.wait_ge(self.S[f'wv{par}'], wt)
                        if g is not None:
                            te.wait_ge(self.S['dv'], g)
                        for c in range(ND):
                            ins = te.matmul(pmm[:], aTv[:, c, 128 * t:128 * (t + 1)],
                                            wvv[par][:, c, :],
                                            start=(c == 0), stop=(c == ND - 1))
                        self.inc(ins, 'pe')
                    self.blk.tensor(fmm)
                    tok_mm = self.n['pe']
                    if t == NT - 1:
                        wv_guard[par] = tok_mm

                    def fse(v, pmm=pmm, ppar=ppar, tok_mm=tok_mm, g=vst_guard[ppar]):
                        v.wait_ge(self.S['pe'], tok_mm)
                        if g is not None:
                            v.wait_ge(self.S[f'vs{ppar}'], g)
                        ins = v.tensor_copy(vst[ppar][:], pmm[:])
                        self.inc(ins, 'dv', eng=v)
                    self.blk.vector(fse)
                    tok_ev = self.n['dv']
                    psgv[ppar] = tok_ev

                    def fst(sp, os=os, t=t, ppar=ppar, tok_ev=tok_ev):
                        sp.wait_ge(self.S['dv'], tok_ev)
                        tok = self.dma(sp, f'vs{ppar}',
                                       self.v_s[128 * t:128 * (t + 1),
                                                512 * os:512 * (os + 1)],
                                       vst[ppar][:])
                        vst_guard[ppar] = tok
                    self.blk.scalar(fst)
        self.quiesce()

    def p_attn(self, x_in, x_out):
        NT, T = self.NT, self.T
        with ExitStack() as es:
            oT = self.sb("p5_oT", [128, ND * T], BF16, es)
            oTv = oT[:].rearrange("p (c tt) -> p c tt", tt=T)
            ktb = [self.sb(f"p5_kt{i}", [128, ND * 128], BF16, es) for i in range(2)]
            qtb = [self.sb(f"p5_qt{i}", [128, ND * 128], BF16, es) for i in range(2)]
            vb = [self.sb(f"p5_v{i}", [128, D], BF16, es) for i in range(2)]
            rp = self.sb("p5_rp", [128, 128], F32, es)
            rc = self.sb("p5_rc", [128, 128], F32, es)
            tsc = self.sb("p5_tsc", [128, 128], F32, es)
            atp = self.sb("p5_atp", [128, 128], BF16, es)
            atc = self.sb("p5_atc", [128, 128], BF16, es)
            ktv = [k[:].rearrange("p (c j) -> p c j", j=128) for k in ktb]
            qtv = [k[:].rearrange("p (c j) -> p c j", j=128) for k in qtb]

            kt_tok = [None, None]
            qt_tok = [None, None]
            v_tok = [None, None]
            last_av = [None]
            relu_guard = [None]
            evac_guard = [None, None]

            def loadw(w):
                par = w % 2

                def fq(sq, w=w, par=par, g=last_av[0]):
                    if g is not None:
                        sq.wait_ge(self.S['pe'], g)
                    self.dma(sq, f'kv{par}', ktv[par], self.kt_s[w])
                    self.dma(sq, f'kv{par}', qtv[par], self.qt_s[w])
                    kt_tok[par] = qt_tok[par] = v_tok[par] = self.dma(
                        sq, f'kv{par}', vb[par][:],
                        self.v_s[128 * w:128 * (w + 1), :])
                self.blk.sync(fq)

            loadw(0)
            for w in range(NT):
                if 'p5' in SERIAL:
                    self.quiesce()
                par = w % 2
                havep = w > 0

                def fsim(te, w=w, par=par, havep=havep, rg=relu_guard[0],
                         qt=qt_tok[par], kt=kt_tok[par]):
                    te.wait_ge(self.S[f'kv{par}'], qt)
                    if rg is not None:
                        te.wait_ge(self.S['ac'], rg)
                    if havep:
                        for c in range(ND):
                            te.matmul(self.psS1[:, 0:128], ktv[1 - par][:, c, :],
                                      qtv[par][:, c, :], start=(c == 0),
                                      stop=(c == ND - 1))
                    for c in range(ND):
                        ins = te.matmul(self.psS2[:, 0:128], ktv[par][:, c, :],
                                        qtv[par][:, c, :], start=(c == 0),
                                        stop=(c == ND - 1))
                    self.inc(ins, 'pe')
                self.blk.tensor(fsim)
                tok_sim = self.n['pe']

                def frelu(s, havep=havep, tok_sim=tok_sim, ag=last_av[0]):
                    s.wait_ge(self.S['pe'], tok_sim)
                    if ag is not None:
                        s.wait_ge(self.S['pe'], ag)
                    if havep:
                        s.activation(rp[:], self.psS1[:, 0:128], AFT.Relu)
                    ins = s.activation(rc[:], self.psS2[:, 0:128], AFT.Relu)
                    self.inc(ins, 'ac')
                self.blk.scalar(frelu)
                relu_guard[0] = self.n['ac']
                tok_relu = self.n['ac']

                def fmask(v, havep=havep, tok_relu=tok_relu):
                    v.wait_ge(self.S['ac'], tok_relu)
                    if havep:
                        v.tensor_tensor(tsc[:], rp[:], self.maskp[:], ALU.mult)
                        v.drain()
                        v.tensor_tensor(atp[:], tsc[:], rp[:], ALU.mult)
                        v.drain()
                    v.tensor_tensor(tsc[:], rc[:], self.maskc[:], ALU.mult)
                    v.drain()
                    ins = v.tensor_tensor(atc[:], tsc[:], rc[:], ALU.mult)
                    v.drain()
                    self.inc(ins, 'dv', eng=v)
                self.blk.vector(fmask)
                tok_at = self.n['dv']

                for grp in range(4):
                    psv = (self.psV1, self.psV2)[grp % 2]
                    gpar = grp % 2

                    def fav(te, par=par, havep=havep, grp=grp, psv=psv,
                            tok_at=tok_at, vt=v_tok[par], ev=evac_guard[grp % 2]):
                        te.wait_ge(self.S['dv'], tok_at)
                        te.wait_ge(self.S[f'kv{par}'], vt)
                        if ev is not None:
                            te.wait_ge(self.S['dv'], ev)
                        for j in range(4):
                            c = 4 * grp + j
                            if havep:
                                te.matmul(psv[:, 128 * j:128 * (j + 1)],
                                          vb[1 - par][:, 128 * c:128 * (c + 1)],
                                          atp[:], start=True, stop=False)
                            ins = te.matmul(psv[:, 128 * j:128 * (j + 1)],
                                            vb[par][:, 128 * c:128 * (c + 1)],
                                            atc[:], start=(not havep), stop=True)
                        self.inc(ins, 'pe')
                    self.blk.tensor(fav)
                    tok_av = self.n['pe']
                    if grp == 3:
                        last_av[0] = tok_av

                    def fve(v, w=w, grp=grp, psv=psv, gpar=gpar, tok_av=tok_av):
                        v.wait_ge(self.S['pe'], tok_av)
                        ins = v.tensor_copy(
                            oTv[:, 4 * grp:4 * grp + 4, 128 * w:128 * (w + 1)],
                            psv[:].rearrange("p (j tt) -> p j tt", tt=128))
                        self.inc(ins, 'dv', eng=v)
                    self.blk.vector(fve)
                    evac_guard[gpar] = self.n['dv']
                if w + 1 < NT:
                    loadw(w + 1)
            tok_oT = self.n['dv']
            if self.debug:
                def fdbg(sp):
                    sp.wait_ge(self.S['dv'], tok_oT)
                    self.dma(sp, 'dbg', self.dbg_oT[:], oT[:])
                self.blk.scalar(fdbg)

            wob = [self.sb(f"p6_wo{i}", [128, ND * 512], BF16, es) for i in range(2)]
            wov = [w[:].rearrange("p (c o) -> p c o", o=512) for w in wob]
            xpc = [self.sb(f"p6_xp{i}", [128, 512], F32, es) for i in range(2)]
            xo = [self.sb(f"p6_xo{i}", [128, 512], F32, es) for i in range(2)]
            wo_tok = [None, None]
            wo_guard = [None, None]
            xo_guard = [None, None]
            xpc_guard = [None, None]
            psg6 = [None, None]

            def loadwo(os):
                par = os % 2

                def fg(g, os=os, par=par, gg=wo_guard[par]):
                    if gg is not None:
                        g.wait_ge(self.S['pe'], gg)
                    wo_tok[par] = self.dma(g, f'wo{par}', wov[par], self.woS[os])
                self.blk.gpsimd(fg)

            loadwo(0)
            for os in range(4):
                par = os % 2
                if os + 1 < 4:
                    loadwo(os + 1)
                for t in range(NT):
                    if 'p6' in SERIAL:
                        self.quiesce()
                    if 'p6pe' in SERIAL:
                        snap = dict(self.n)
                        def fpe(te, snap=snap):
                            for s_, v_ in snap.items():
                                if v_:
                                    te.wait_ge(self.S[s_], v_)
                        self.blk.tensor(fpe)
                    pmm = (self.psA, self.psB)[t % 2]
                    ppar = t % 2

                    def fsx(sq, os=os, t=t, ppar=ppar, g=xpc_guard[ppar]):
                        if g is not None:
                            sq.wait_ge(self.S['dv'], g)
                        self.dma(sq, f'xp{ppar}', xpc[ppar][:],
                                 x_in[128 * t:128 * (t + 1), 512 * os:512 * (os + 1)])
                    self.blk.sync(fsx)
                    tok_xp = self.n[f'xp{ppar}']

                    def fmm(te, os=os, t=t, pmm=pmm, ppar=ppar, par=par,
                            wt=wo_tok[par], g=psg6[ppar]):
                        te.wait_ge(self.S[f'wo{par}'], wt)
                        if os == 0 and t == 0:
                            te.wait_ge(self.S['dv'], tok_oT)
                        if g is not None:
                            te.wait_ge(self.S['dv'], g)
                        for c in range(ND):
                            ins = te.matmul(pmm[:], oTv[:, c, 128 * t:128 * (t + 1)],
                                            wov[par][:, c, :],
                                            start=(c == 0), stop=(c == ND - 1))
                        self.inc(ins, 'pe')
                    self.blk.tensor(fmm)
                    tok_mm = self.n['pe']
                    if t == NT - 1:
                        wo_guard[par] = tok_mm

                    def fvu(v, pmm=pmm, ppar=ppar, tok_mm=tok_mm, tok_xp=tok_xp,
                            g=xo_guard[ppar]):
                        v.wait_ge(self.S['pe'], tok_mm)
                        v.wait_ge(self.S[f'xp{ppar}'], tok_xp)
                        if g is not None:
                            v.wait_ge(self.S[f'p{ppar}'], g)
                        ins = v.scalar_tensor_tensor(xo[ppar][:], pmm[:], 1.0,
                                                     xpc[ppar][:], ALU.mult, ALU.add)
                        self.inc(ins, 'dv', eng=v)
                    self.blk.vector(fvu)
                    tok_u = self.n['dv']
                    psg6[ppar] = tok_u
                    xpc_guard[ppar] = tok_u

                    def fso(sp, os=os, t=t, ppar=ppar, tok_u=tok_u):
                        sp.wait_ge(self.S['dv'], tok_u)
                        tok = self.dma(sp, f'p{ppar}',
                                       x_out[128 * t:128 * (t + 1),
                                             512 * os:512 * (os + 1)],
                                       xo[ppar][:])
                        xo_guard[ppar] = tok
                    self.blk.scalar(fso)
        self.quiesce()

    def build(self):
        self.p0()
        with ExitStack() as ffn_es:
            q1T = self.sb("q1T_a", [128, ND * self.T], BF16, ffn_es)
            self.p_h0(self.x0, q1T)
            self.p_m1(True, q1T)
        self.p_m2(self.x0, self.x1s, 'p')
        self.p_qkv(self.x1s)
        self.p_attn(self.x1s, self.x2s)
        with ExitStack() as ffn_es:
            q1T = self.sb("q1T_b", [128, ND * self.T], BF16, ffn_es)
            self.p_h0(self.x2s, q1T)
            self.p_m1(False, q1T)
        self.p_m2(self.x2s, self.outd, 'o')
        snap = dict(self.n)

        def fin(sq, snap=snap):
            for s, v in snap.items():
                if v:
                    sq.wait_ge(self.S[s], v)
        self.blk.sync(fin)

    def finish(self):
        self.es.close()
        return self.nc


def build_program(T=2048, debug=False):
    mk = MK(T, debug=debug)
    mk.build()
    return mk.finish()


def host_prep(x_full, w_attn, w_o, ln_w, ln_b, W1, b1, W2, b2):
    f = np.float32
    bf = ml_dtypes.bfloat16
    wa = w_attn.astype(f, copy=False)
    common = {
        "watS": np.ascontiguousarray(
            wa[:4096].reshape(32, 128, 16, 128).transpose(0, 3, 2, 1)),
        "wvS": np.ascontiguousarray(
            wa[4096:].reshape(4, 512, 16, 128).transpose(0, 3, 2, 1)),
        "woS": np.ascontiguousarray(
            w_o.astype(f, copy=False).reshape(4, 512, 16, 128).transpose(0, 3, 2, 1)),
        "w1": np.ascontiguousarray(W1.astype(f, copy=False)),
        "w2": np.ascontiguousarray(W2.astype(f, copy=False)),
        "lnw128": np.ascontiguousarray(np.broadcast_to(ln_w[None, :], (128, D)).astype(f)),
        "lnb128": np.ascontiguousarray(np.broadcast_to(ln_b[None, :], (128, D)).astype(f)),
        "b1v": np.ascontiguousarray(b1[None, :].astype(f)),
        "b2v": np.ascontiguousarray(b2[None, :].astype(f)),
        "identd": np.eye(128, dtype=bf),
        "ones2d": np.ones((2, 128), dtype=bf),
        "maskpd": np.tril(np.ones((128, 128), dtype=f)),
        "maskcd": np.triu(np.ones((128, 128), dtype=f)),
    }
    xs = [np.ascontiguousarray(x_full[i].astype(f, copy=False))
          for i in range(x_full.shape[0])]
    return common, xs


def kernel(L0_attn_x, L0_attn_w_attn, L0_attn_w_o, L0_ffn_ln_w, L0_ffn_ln_b,
           L0_ffn_W1, L0_ffn_b1, L0_ffn_W2, L0_ffn_b2):
    x = np.asarray(L0_attn_x)
    B, T, _ = x.shape
    common, xs = host_prep(x, np.asarray(L0_attn_w_attn), np.asarray(L0_attn_w_o),
                           np.asarray(L0_ffn_ln_w), np.asarray(L0_ffn_ln_b),
                           np.asarray(L0_ffn_W1), np.asarray(L0_ffn_b1),
                           np.asarray(L0_ffn_W2), np.asarray(L0_ffn_b2))
    nc = build_program(T)
    in_maps = [dict(common, x0=xs[i]) for i in range(B)]
    res = run_bass_kernel_spmd(nc, in_maps, core_ids=list(range(B)))
    return np.stack([np.asarray(r["out"]) for r in res.results], axis=0)


# version 10
# speedup vs baseline: 1.7431x; 1.5993x over previous
import numpy as np
import ml_dtypes
from contextlib import ExitStack
import os as _os
import concourse.bass as bass
import concourse.mybir as mybir
from concourse.bass_utils import run_bass_kernel_spmd

SERIAL = set((_os.environ.get("MK_SERIAL") or "").split(","))

F32 = mybir.dt.float32
BF16 = mybir.dt.bfloat16
I32 = mybir.dt.int32
I8 = mybir.dt.int8
ALU = mybir.AluOpType
AFT = mybir.ActivationFunctionType
AXX = mybir.AxisListType.X

D = 2048
FF = 8192
ND = D // 128
NF = FF // 128
NSL = FF // 512
EPS = 1e-5
EXPMASK = 0x7F800000


class MK:
    def __init__(self, T=2048, debug=False):
        self.T = T
        self.debug = debug
        self.NT = T // 128
        nc = bass.Bass("TRN2", target_bir_lowering=False, detect_race_conditions=False)
        self.nc = nc
        self.es = ExitStack()
        dr = lambda n, sh, dt, k: nc.dram_tensor(n, sh, dt, kind=k).ap()
        self.x0 = dr("x0", [T, D], F32, "ExternalInput")
        self.watd = dr("wat", [3 * D, D], F32, "ExternalInput")
        self.wod = dr("wo", [D, D], F32, "ExternalInput")
        self.w1 = dr("w1", [FF, D], F32, "ExternalInput")
        self.w2 = dr("w2", [D, FF], F32, "ExternalInput")
        self.lnw128 = dr("lnw128", [128, D], F32, "ExternalInput")
        self.lnb128 = dr("lnb128", [128, D], F32, "ExternalInput")
        self.b1v = dr("b1v", [1, FF], F32, "ExternalInput")
        self.b2v = dr("b2v", [1, D], F32, "ExternalInput")
        self.ones2d = dr("ones2d", [2, 128], BF16, "ExternalInput")
        self.maskpcd = dr("maskpcd", [128, 256], F32, "ExternalInput")
        self.epscd = dr("epscd", [128, 1], F32, "ExternalInput")
        self.outd = dr("out", [T, D], F32, "ExternalOutput")
        dk = "ExternalOutput" if debug else "Internal"
        self.x1s = dr("x1s", [T, D], F32, dk)
        self.x2s = dr("x2s", [T, D], F32, dk)
        self.q1_s = dr("q1_s", [T, D], BF16, dk)
        self.q2_s = dr("q2_s", [T, FF], BF16, dk)
        self.w1q_s = dr("w1q_s", [FF, D], BF16, dk)
        self.w2q_s = dr("w2q_s", [D, FF], BF16, dk)
        self.at_s = dr("at_s", [T, D], BF16, dk)
        self.qkv_s = dr("qkv_s", [T, 3 * D], BF16, dk)
        self.watb_s = dr("watb_s", [3 * D, D], BF16, "Internal")
        self.wob_s = dr("wob_s", [D, D], BF16, "Internal")
        self.b1hl_s = dr("b1hl_s", [2, FF], BF16, "Internal")
        self.b2hl_s = dr("b2hl_s", [2, D], BF16, "Internal")
        self.S = {}
        self.n = {}
        sb = self.sb
        self.ones2 = sb("ones2", [2, 128], BF16)
        self.maskpc = sb("maskpc", [128, 256], F32)
        self.epsc = sb("epsc", [128, 1], F32)
        self.lnsave = sb("lnsave", [128, 2 * self.NT], F32)
        self.qam = [sb(f"qam_{i}", [128, 64], F32) for i in range(2)]
        self.qse = [sb(f"qse_{i}", [128, 64], F32) for i in range(2)]
        self.qiv = [sb(f"qiv_{i}", [128, 64], F32) for i in range(2)]
        self.qsc = [sb(f"qsc_{i}", [128, 64], F32) for i in range(2)]
        self.st24 = [sb(f"st24_{i}", [128, 24], F32) for i in range(2)]
        self.mv2 = [sb(f"mv2_{i}", [128, 2], F32) for i in range(2)]
        self.rt = [sb(f"rt_{i}", [128, 1], F32) for i in range(2)]
        self.rt2 = [sb(f"rt2_{i}", [128, 1], F32) for i in range(2)]
        self.q8 = sb("q8", [128, D], I8)
        ps = lambda n: self.es.enter_context(nc.psum_tensor(n, [128, 512], F32))
        self.psA = ps("psA"); self.psB = ps("psB")
        self.psC = ps("psC"); self.psD = ps("psD")
        self.psS1 = ps("psS1"); self.psS2 = ps("psS2")
        self.psV1 = ps("psV1"); self.psV2 = ps("psV2")
        self.blk = self.es.enter_context(nc.Block())

    def sem(self, name):
        if name not in self.S:
            self.S[name] = self.es.enter_context(self.nc.semaphore("sem_" + name))
            self.n[name] = 0
        return self.S[name]

    def sb(self, name, shape, dt, es=None):
        self._uc = getattr(self, '_uc', 0) + 1
        return (es or self.es).enter_context(
            self.nc.sbuf_tensor(f"{name}_u{self._uc}", shape, dt))

    def inc(self, instr, sem, k=1, eng=None):
        self.sem(sem)
        if eng is not None:
            instr = eng.drain()
        instr.then_inc(self.S[sem], k)
        self.n[sem] += k
        return self.n[sem]

    def dma(self, eng, sem, out, in_, transpose=False):
        self.sem(sem)
        if transpose:
            ins = eng.dma_start(out=out, in_=in_, transpose=True)
        else:
            ins = eng.dma_start(out=out, in_=in_)
        return self.inc(ins, sem, 16)

    def quiesce(self):
        snap = dict(self.n)
        for en in ("sync", "scalar", "vector", "tensor", "gpsimd"):
            def f(e, snap=snap):
                for s, v in snap.items():
                    if v:
                        e.wait_ge(self.S[s], v)
            getattr(self.blk, en)(f)

    def emit_quant8(self, v, src, dst, n, par):
        nb = n // 32
        am = self.qam[par][:, 0:nb]
        se = self.qse[par][:, 0:nb]
        iv = self.qiv[par][:, 0:nb]
        sc = self.qsc[par][:, 0:nb]
        q8 = self.q8[:, 0:n]
        bv = lambda ap: ap.rearrange("p (nb b) -> p nb b", b=32)
        bb = lambda ap: ap.unsqueeze(2).broadcast_to([128, nb, 32])
        v.tensor_reduce(am, bv(src), AXX, ALU.max, apply_absolute_value=True)
        v.drain()
        v.tensor_scalar(se.bitcast(I32), am.bitcast(I32), EXPMASK, None,
                        ALU.bitwise_and)
        v.drain()
        v.reciprocal(iv, se)
        v.tensor_scalar(sc, se, 1.0 / 64.0, None, ALU.mult)
        v.drain()
        v.scalar_tensor_tensor(bv(q8), bb(iv), 64.0, bv(src), ALU.mult, ALU.mult)
        return v.tensor_tensor(bv(dst), bv(q8), bb(sc), ALU.mult)

    def emit_ln0(self, src_ap, par, t, tok_src_sem, tok_src):
        nt = {}

        def fv(v, t=t, par=par):
            v.wait_ge(self.S[tok_src_sem], tok_src)
            for c in range(4):
                v.bn_stats(self.st24[par][:, 6 * c:6 * c + 6],
                           src_ap[:, 512 * c:512 * (c + 1)])
            v.drain()
            ins = v.bn_aggr(self.lnsave[:, 2 * t:2 * t + 2], self.st24[par][:])
            nt['v'] = self.inc(ins, 'dv', eng=v)
        self.blk.vector(fv)

        def fs(s, t=t, par=par):
            s.wait_ge(self.S['dv'], nt['v'])
            ins = s.activation(self.rt[par][:], self.lnsave[:, 2 * t + 1:2 * t + 2],
                               AFT.Sqrt, bias=self.epsc[:, 0:1])
            nt['a'] = self.inc(ins, 'ac')
        self.blk.scalar(fs)
        return nt

    def p0(self):
        nt = {}

        def fg(g):
            self.dma(g, 'wcst', self.watb_s[:], self.watd[:])
            self.dma(g, 'wcst', self.wob_s[:], self.wod[:])
        self.blk.gpsimd(fg)

        with ExitStack() as es:
            b1f = self.sb("b1f", [1, FF], F32, es)
            b2f = self.sb("b2f", [1, D], F32, es)
            b1h = self.sb("b1h", [1, FF], BF16, es)
            b2h = self.sb("b2h", [1, D], BF16, es)
            b1l = self.sb("b1l", [1, FF], BF16, es)
            b2l = self.sb("b2l", [1, D], BF16, es)

            def fsync(sq):
                self.dma(sq, 'cst', self.ones2[:], self.ones2d[:])
                self.dma(sq, 'cst', self.maskpc[:], self.maskpcd[:])
                self.dma(sq, 'cst', self.epsc[:], self.epscd[:])
                self.dma(sq, 'cst', b1f[:], self.b1v[:])
                nt['b'] = self.dma(sq, 'cst', b2f[:], self.b2v[:])
            self.blk.sync(fsync)

            def fv(v):
                v.wait_ge(self.S['cst'], nt['b'])
                v.tensor_copy(b1h[:], b1f[:])
                v.tensor_copy(b2h[:], b2f[:])
                v.drain()
                v.tensor_tensor(b1f[:], b1f[:], b1h[:], ALU.subtract)
                v.tensor_tensor(b2f[:], b2f[:], b2h[:], ALU.subtract)
                v.drain()
                v.tensor_copy(b1l[:], b1f[:])
                ins = v.tensor_copy(b2l[:], b2f[:])
                nt['v'] = self.inc(ins, 'dv', eng=v)
            self.blk.vector(fv)

            def fst(sp):
                sp.wait_ge(self.S['dv'], nt['v'])
                self.dma(sp, 'cst', self.b1hl_s[0:1, :], b1h[:])
                self.dma(sp, 'cst', self.b1hl_s[1:2, :], b1l[:])
                self.dma(sp, 'cst', self.b2hl_s[0:1, :], b2h[:])
                self.dma(sp, 'cst', self.b2hl_s[1:2, :], b2l[:])
            self.blk.scalar(fst)
        self.quiesce()

    def p_ffn_ab(self, first, x_in):
        NT, T = self.NT, self.T
        with ExitStack() as es:
            lnw = self.sb("lnw", [128, D], F32, es)
            lnb = self.sb("lnb", [128, D], F32, es)
            b1hl = self.sb("b1hl", [2, FF], BF16, es)
            self.q1T = self.sb("q1T", [128, ND * T], BF16, es)
            q1Tv = self.q1T[:].rearrange("p (c t) -> p c t", t=T)
            wbuf = [self.sb(f"wbuf{i}", [128, ND * 512], BF16, es) for i in range(2)]
            wbufv = [w[:].rearrange("p (c o) -> p c o", o=512) for w in wbuf]
            xa = [self.sb(f"xa{i}", [128, D], F32, es) for i in range(2)]
            xbt = self.sb("xbt", [128, D], BF16, es)
            h0 = self.sb("h0", [128, D], BF16, es)
            q1b = [self.sb(f"q1b{i}", [128, D], BF16, es) for i in range(2)]
            wn = [self.sb(f"wn{i}", [128, D], F32, es) for i in range(2)]
            wqv = [w[:].bitcast(BF16)[:, 0:D] for w in wn]
            gsb = [self.sb(f"gsb{i}", [128, 512], BF16, es) for i in range(2)]
            q2b = [self.sb(f"q2b{i}", [128, 512], BF16, es) for i in range(2)]

            def fsync(sq):
                self.dma(sq, 'cst', lnw[:], self.lnw128[:])
                self.dma(sq, 'cst', lnb[:], self.lnb128[:])
                self.dma(sq, 'cst', b1hl[:], self.b1hl_s[:])
            self.blk.sync(fsync)
            tok_cst = self.n['cst']

            guard_xa = [None, None]
            guard_q1b = [None, None]
            m1tok_sl0 = [None] * NT
            q1l_tok = [None] * NT

            def stageA(t):
                par = t % 2

                def fload(sq, t=t, par=par, g=guard_xa[par]):
                    if g is not None:
                        sq.wait_ge(self.S['dv'], g)
                    self.dma(sq, f'xa{par}', xa[par][:],
                             x_in[128 * t:128 * (t + 1), :])
                self.blk.sync(fload)
                tok_x = self.n[f'xa{par}']

                nt0 = self.emit_ln0(xa[par][:], par, t, f'xa{par}', tok_x)
                nt = {}

                def fvB(v, t=t, par=par, nt0=nt0):
                    v.wait_ge(self.S['ac'], nt0['a'])
                    v.reciprocal(self.lnsave[:, 2 * t + 1:2 * t + 2], self.rt[par][:])
                    v.drain()
                    v.tensor_scalar(xbt[:], xa[par][:],
                                    self.lnsave[:, 2 * t:2 * t + 1],
                                    self.lnsave[:, 2 * t + 1:2 * t + 2],
                                    ALU.subtract, ALU.mult)
                    v.drain()
                    for c in range(4):
                        v.bn_stats(self.st24[par][:, 6 * c:6 * c + 6],
                                   xbt[:, 512 * c:512 * (c + 1)])
                    v.drain()
                    ins = v.bn_aggr(self.mv2[par][:], self.st24[par][:])
                    nt['v'] = self.inc(ins, 'dv', eng=v)
                self.blk.vector(fvB)

                def fsB(s, par=par):
                    s.wait_ge(self.S['dv'], nt['v'])
                    ins = s.activation(self.rt2[par][:], self.mv2[par][:, 1:2],
                                       AFT.Sqrt, bias=self.epsc[:, 0:1])
                    nt['a'] = self.inc(ins, 'ac')
                self.blk.scalar(fsB)

                def fvC(v, t=t, par=par, gq=guard_q1b[par]):
                    v.wait_ge(self.S['ac'], nt['a'])
                    if gq is not None:
                        v.wait_ge(self.S[f'q1s{par}'], gq)
                    v.wait_ge(self.S['cst'], tok_cst)
                    v.reciprocal(self.rt2[par][:], self.rt2[par][:])
                    v.drain()
                    ut = xa[par][:]
                    v.tensor_scalar(ut, xbt[:], self.mv2[par][:, 0:1],
                                    self.rt2[par][:], ALU.subtract, ALU.mult)
                    v.tensor_tensor(ut, ut, lnw[:], ALU.mult)
                    v.tensor_tensor(h0[:], ut, lnb[:], ALU.add)
                    ins = self.emit_quant8(v, h0[:], q1b[par][:], D, par)
                    nt['q'] = self.inc(ins, 'dv', eng=v)
                self.blk.vector(fvC)
                guard_xa[par] = nt['q']

                def fst(sp, t=t, par=par):
                    sp.wait_ge(self.S['dv'], nt['q'])
                    self.dma(sp, f'q1s{par}', self.q1_s[128 * t:128 * (t + 1), :],
                             q1b[par][:])
                self.blk.scalar(fst)
                guard_q1b[par] = self.n[f'q1s{par}']
                tok_q1st = self.n[f'q1s{par}']

                def fldT(sq, t=t, par=par, tok_q1st=tok_q1st,
                         g=(m1tok_sl0[t - 2] if t >= 2 else None)):
                    sq.wait_ge(self.S[f'q1s{par}'], tok_q1st)
                    if g is not None:
                        sq.wait_ge(self.S['pe'], g)
                    self.dma(sq, f'q1l{par}',
                             q1Tv[:, :, 128 * t:128 * (t + 1)],
                             self.q1_s[128 * t:128 * (t + 1), :].rearrange(
                                 "t (c p) -> t c p", p=128), transpose=True)
                    q1l_tok[t] = self.n[f'q1l{par}']
                self.blk.sync(fldT)

            w1_ready = [None, None]
            mm_done = [None, None]
            wn_guard = [None, None]
            wcnt = [0]

            def w_tile(kind, idx):
                wpar = wcnt[0] % 2
                item_i = wcnt[0]
                wcnt[0] += 1
                ssem = f'{kind}s{wpar}'
                self.sem(ssem)
                if kind == 'w1':
                    sl, rt = idx
                    r0 = 512 * sl + 128 * rt
                    src = self.w1[r0:r0 + 128, :]
                    dst = self.w1q_s[r0:r0 + 128, :]
                else:
                    j, qc = idx // 4, idx % 4
                    src = self.w2[128 * j:128 * (j + 1), 2048 * qc:2048 * (qc + 1)]
                    dst = self.w2q_s[128 * j:128 * (j + 1), 2048 * qc:2048 * (qc + 1)]

                self.sem(f'w1s{wpar}')
                self.sem(f'w2s{wpar}')

                def fload(sq, wpar=wpar, src=src, g=wn_guard[wpar],
                          g1=self.n[f'w1s{wpar}'], g2=self.n[f'w2s{wpar}']):
                    if g is not None:
                        sq.wait_ge(self.S['dv'], g)
                    if g1:
                        sq.wait_ge(self.S[f'w1s{wpar}'], g1)
                    if g2:
                        sq.wait_ge(self.S[f'w2s{wpar}'], g2)
                    self.dma(sq, f'wn{wpar}', wn[wpar][:], src)
                self.blk.sync(fload)
                tok_ld = self.n[f'wn{wpar}']
                nt = {}

                def fv(v, wpar=wpar, tok_ld=tok_ld):
                    v.wait_ge(self.S[f'wn{wpar}'], tok_ld)
                    ins = self.emit_quant8(v, wn[wpar][:], wqv[wpar], D, wpar)
                    nt['q'] = self.inc(ins, 'dv', eng=v)
                self.blk.vector(fv)
                wn_guard[wpar] = nt['q']

                def fst(sp, wpar=wpar, dst=dst, ssem=ssem):
                    sp.wait_ge(self.S['dv'], nt['q'])
                    self.dma(sp, ssem, dst, wqv[wpar])
                self.blk.scalar(fst)

            def w1_load(sl):
                bpar = sl % 2
                w0 = self.n.get('w1s0', 0)
                w1c = self.n.get('w1s1', 0)

                def fload(sq, sl=sl, bpar=bpar, mmd=mm_done[bpar], w0=w0, w1c=w1c):
                    if first:
                        if w0:
                            sq.wait_ge(self.S['w1s0'], w0)
                        if w1c:
                            sq.wait_ge(self.S['w1s1'], w1c)
                    if mmd is not None:
                        sq.wait_ge(self.S['pe'], mmd)
                    r0 = 512 * sl
                    self.dma(sq, f'wb{bpar}', wbufv[bpar],
                             self.w1q_s[r0:r0 + 512, :].rearrange(
                                 "o (c p) -> o c p", p=128), transpose=True)
                    w1_ready[bpar] = self.n[f'wb{bpar}']
                self.blk.sync(fload)

            psm1 = [self.psA, self.psB, self.psC, self.psD]
            gelu_done = [None] * 4
            qpair_guard = [None]
            q2st_done = [None, None]
            q2cnt = [0]

            def m1_tile(sl, t):
                par4 = t % 4
                bpar = sl % 2
                pmm = psm1[par4]

                def fmm(te, sl=sl, t=t, bpar=bpar, pmm=pmm, wtok=w1_ready[bpar],
                        g=gelu_done[par4], ql=q1l_tok[t], qpar=t % 2):
                    te.wait_ge(self.S['cst'], tok_cst)
                    te.wait_ge(self.S[f'wb{bpar}'], wtok)
                    te.wait_ge(self.S[f'q1l{qpar}'], ql)
                    if g is not None:
                        te.wait_ge(self.S['ac'], g)
                    for c in range(ND):
                        te.matmul(pmm[:], q1Tv[:, c, 128 * t:128 * (t + 1)],
                                  wbufv[bpar][:, c, :], start=(c == 0), stop=False)
                    ins = te.matmul(pmm[:], self.ones2[:],
                                    b1hl[:, 512 * sl:512 * (sl + 1)],
                                    start=False, stop=True)
                    self.inc(ins, 'pe')
                self.blk.tensor(fmm)
                tok_mm = self.n['pe']
                if t == NT - 1:
                    mm_done[bpar] = tok_mm
                if sl == 0:
                    m1tok_sl0[t] = tok_mm

                if t % 2 == 1:
                    nt = {}

                    def fg(s, t=t, tok_mm=tok_mm, g=qpair_guard[0]):
                        s.wait_ge(self.S['pe'], tok_mm)
                        if g is not None:
                            s.wait_ge(self.S['dv'], g)
                        s.activation(gsb[0][:], psm1[(t - 1) % 4][:],
                                     AFT.Gelu_apprx_tanh)
                        ins = s.activation(gsb[1][:], psm1[t % 4][:],
                                           AFT.Gelu_apprx_tanh)
                        nt['g'] = self.inc(ins, 'ac')
                    self.blk.scalar(fg)
                    gelu_done[(t - 1) % 4] = nt['g']
                    gelu_done[t % 4] = nt['g']

                    for k, tt in enumerate((t - 1, t)):
                        qpar = q2cnt[0] % 2
                        q2cnt[0] += 1
                        ntq = {}

                        def fq(v, k=k, qpar=qpar, g=nt['g'], gst=q2st_done[qpar]):
                            v.wait_ge(self.S['ac'], g)
                            if gst is not None:
                                v.wait_ge(self.S[f'q2s{qpar}'], gst)
                            ins = self.emit_quant8(v, gsb[k][:], q2b[qpar][:],
                                                   512, qpar)
                            ntq['q'] = self.inc(ins, 'dv', eng=v)
                        self.blk.vector(fq)
                        if k == 1:
                            qpair_guard[0] = ntq['q']

                        def fst(sp, sl=sl, tt=tt, qpar=qpar, tok=ntq['q']):
                            sp.wait_ge(self.S['dv'], tok)
                            self.dma(sp, f'q2s{qpar}',
                                     self.q2_s[128 * tt:128 * (tt + 1),
                                               512 * sl:512 * (sl + 1)],
                                     q2b[qpar][:])
                        self.blk.scalar(fst)
                        q2st_done[qpar] = self.n[f'q2s{qpar}']

            witems = {sl: [] for sl in range(NSL)}
            if first:
                for sl in range(NSL - 1):
                    witems[sl] += [('w1', (sl + 1, rt)) for rt in range(4)]
                for h in range(64):
                    witems[h // 4].append(('w2', h))

            def emit_witem(it):
                kind, a = it
                w_tile(kind, a)
                if kind == 'w1' and a[1] == 3:
                    w1_load(a[0])

            wl0 = witems[0]
            done0 = 0
            for t in range(NT):
                if 'pa' in SERIAL:
                    self.quiesce()
                stageA(t)
                if t == 0:
                    if first:
                        for rt in range(4):
                            w_tile('w1', (0, rt))
                    w1_load(0)
                else:
                    want = (t * len(wl0) + NT - 2) // (NT - 1)
                    while done0 < min(want, len(wl0)):
                        emit_witem(wl0[done0])
                        done0 += 1
                if t >= 1:
                    m1_tile(0, t - 1)
            m1_tile(0, NT - 1)
            while done0 < len(wl0):
                emit_witem(wl0[done0])
                done0 += 1
            if not first:
                w1_load(1)

            for sl in range(1, NSL):
                wl = witems[sl]
                done = 0
                for t in range(NT):
                    if 'pb' in SERIAL:
                        self.quiesce()
                    if not first and t == 0 and sl + 1 < NSL:
                        w1_load(sl + 1)
                    want = ((t + 1) * len(wl) + NT - 1) // NT
                    while done < min(want, len(wl)):
                        emit_witem(wl[done])
                        done += 1
                    m1_tile(sl, t)
                while done < len(wl):
                    emit_witem(wl[done])
                    done += 1
        self.quiesce()

    def p_ffn_c(self, first, x_in, x_out, out_sem):
        NT, T = self.NT, self.T
        with ExitStack() as es:
            ring = self.sb("ring", [128, NF * 1024], BF16, es)
            ringv = ring[:].rearrange("p (c d) -> p c d", d=1024)
            b2hl = self.sb("b2hl", [2, D], BF16, es)
            q2t = [self.sb(f"q2t{i}", [128, NF * 128], BF16, es) for i in range(2)]
            q2tv = [r[:].rearrange("p (c t) -> p c t", t=128) for r in q2t]
            xpc = [self.sb(f"xpc{i}", [128, 1024], F32, es) for i in range(2)]
            xbp = [self.sb(f"xbp{i}", [128, 1024], BF16, es) for i in range(2)]
            h2p = self.sb("h2p", [128, 1024], BF16, es)
            xo = [self.sb(f"xo{i}", [128, 1024], F32, es) for i in range(2)]

            def fsync0(sq):
                self.dma(sq, 'cst', b2hl[:], self.b2hl_s[:])
            self.blk.sync(fsync0)
            tok_cst = self.n['cst']

            q2_tok = [None, None]
            q2_guard = [None, None]
            ps_guard = [None, None]
            xpc_guard = [None, None]
            xo_guard = [None, None]
            ring_guard = [None]
            q2s_tot = (self.n.get('q2s0', 0), self.n.get('q2s1', 0))
            w2s_tot = (self.n.get('w2s0', 0), self.n.get('w2s1', 0))

            def ring_load(dh):
                def f(sq, dh=dh, g=ring_guard[0]):
                    if first:
                        if w2s_tot[0]:
                            sq.wait_ge(self.S['w2s0'], w2s_tot[0])
                        if w2s_tot[1]:
                            sq.wait_ge(self.S['w2s1'], w2s_tot[1])
                    if g is not None:
                        sq.wait_ge(self.S['pe'], g)
                    self.dma(sq, 'rg', ringv,
                             self.w2q_s[1024 * dh:1024 * (dh + 1), :].rearrange(
                                 "d (c p) -> d c p", p=128), transpose=True)
                self.blk.sync(f)
                return self.n['rg']

            def q2t_load(t):
                par = t % 2

                def f(sq, t=t, par=par, g=q2_guard[par]):
                    sq.wait_ge(self.S['q2s0'], q2s_tot[0])
                    sq.wait_ge(self.S['q2s1'], q2s_tot[1])
                    if g is not None:
                        sq.wait_ge(self.S['pe'], g)
                    self.dma(sq, f'q2l{par}', q2tv[par],
                             self.q2_s[128 * t:128 * (t + 1), :].rearrange(
                                 "t (c p) -> t c p", p=128), transpose=True)
                    q2_tok[par] = self.n[f'q2l{par}']
                self.blk.sync(f)

            for dh in range(2):
                tok_rg = ring_load(dh)
                if dh == 0:
                    q2t_load(0)
                for t in range(NT):
                    if 'pc' in SERIAL:
                        self.quiesce()
                    par = t % 2
                    pa = (self.psA, self.psS1)[par]
                    pb = (self.psB, self.psS2)[par]
                    if t + 1 < NT:
                        q2t_load(t + 1)
                    elif dh == 0:
                        q2t_load(0)

                    def fmm(te, dh=dh, t=t, par=par, pa=pa, pb=pb, tok_rg=tok_rg,
                            q2r=q2_tok[par], g=ps_guard[par]):
                        te.wait_ge(self.S['rg'], tok_rg)
                        te.wait_ge(self.S[f'q2l{par}'], q2r)
                        te.wait_ge(self.S['cst'], tok_cst)
                        if g is not None:
                            te.wait_ge(self.S['dv'], g)
                        for c in range(NF):
                            te.matmul(pa[:], q2tv[par][:, c, :],
                                      ringv[:, c, 0:512], start=(c == 0), stop=False)
                            te.matmul(pb[:], q2tv[par][:, c, :],
                                      ringv[:, c, 512:1024], start=(c == 0),
                                      stop=False)
                        o0 = 1024 * dh
                        te.matmul(pa[:], self.ones2[:],
                                  b2hl[:, o0:o0 + 512], start=False, stop=True)
                        ins = te.matmul(pb[:], self.ones2[:],
                                        b2hl[:, o0 + 512:o0 + 1024],
                                        start=False, stop=True)
                        self.inc(ins, 'pe')
                    self.blk.tensor(fmm)
                    tok_mm = self.n['pe']
                    q2_guard[par] = tok_mm
                    if t == NT - 1:
                        ring_guard[0] = tok_mm

                    def fsx(sq, dh=dh, t=t, par=par, g=xpc_guard[par]):
                        if g is not None:
                            sq.wait_ge(self.S['dv'], g)
                        self.dma(sq, f'xp{par}', xpc[par][:],
                                 x_in[128 * t:128 * (t + 1),
                                      1024 * dh:1024 * (dh + 1)])
                    self.blk.sync(fsx)
                    tok_xp = self.n[f'xp{par}']
                    nt = {}

                    def fv(v, dh=dh, t=t, par=par, pa=pa, pb=pb, tok_mm=tok_mm,
                           tok_xp=tok_xp, gxo=xo_guard[par]):
                        v.wait_ge(self.S['pe'], tok_mm)
                        v.wait_ge(self.S[f'xp{par}'], tok_xp)
                        if gxo is not None:
                            v.wait_ge(self.S[f'{out_sem}{par}'], gxo)
                        v.tensor_scalar(xbp[par][:], xpc[par][:],
                                        self.lnsave[:, 2 * t:2 * t + 1],
                                        self.lnsave[:, 2 * t + 1:2 * t + 2],
                                        ALU.subtract, ALU.mult)
                        v.tensor_copy(h2p[:, 0:512], pa[:])
                        v.tensor_copy(h2p[:, 512:1024], pb[:])
                        v.drain()
                        v.tensor_tensor(h2p[:], xbp[par][:], h2p[:], ALU.add)
                        v.drain()
                        ins = v.scalar_tensor_tensor(xo[par][:], h2p[:], 0.5,
                                                     xpc[par][:], ALU.mult, ALU.add)
                        nt['u'] = self.inc(ins, 'dv', eng=v)
                    self.blk.vector(fv)
                    ps_guard[par] = nt['u']
                    xpc_guard[par] = nt['u']

                    def fso(sp, dh=dh, t=t, par=par, tok=nt['u']):
                        sp.wait_ge(self.S['dv'], tok)
                        self.dma(sp, f'{out_sem}{par}',
                                 x_out[128 * t:128 * (t + 1),
                                       1024 * dh:1024 * (dh + 1)],
                                 xo[par][:])
                        xo_guard[par] = self.n[f'{out_sem}{par}']
                    self.blk.scalar(fso)
        self.quiesce()

    def p_attn_qkv(self, x_in):
        NT, T = self.NT, self.T
        with ExitStack() as es:
            self.aT = self.sb("aT", [128, ND * T], BF16, es)
            aTv = self.aT[:].rearrange("p (c t) -> p c t", t=T)
            xa = [self.sb(f"xa{i}", [128, D], F32, es) for i in range(2)]
            at = [self.sb(f"at{i}", [128, D], BF16, es) for i in range(2)]
            wsl = [self.sb(f"wsl{i}", [128, ND * 512], BF16, es) for i in range(2)]
            wslv = [w[:].rearrange("p (c o) -> p c o", o=512) for w in wsl]
            qkvb = [self.sb(f"qkvb{i}", [128, 512], BF16, es) for i in range(2)]

            guard_xa = [None, None]
            guard_at = [None, None]
            aTl_tok = [None] * NT
            e0tok = [None] * NT

            def stageD(t):
                par = t % 2

                def fload(sq, t=t, par=par, g=guard_xa[par]):
                    if g is not None:
                        sq.wait_ge(self.S['dv'], g)
                    self.dma(sq, f'xa{par}', xa[par][:],
                             x_in[128 * t:128 * (t + 1), :])
                self.blk.sync(fload)
                tok_x = self.n[f'xa{par}']
                nt0 = self.emit_ln0(xa[par][:], par, t, f'xa{par}', tok_x)
                nt = {}

                def fvB(v, t=t, par=par, nt0=nt0, gq=guard_at[par]):
                    v.wait_ge(self.S['ac'], nt0['a'])
                    if gq is not None:
                        v.wait_ge(self.S[f'ats{par}'], gq)
                    v.reciprocal(self.lnsave[:, 2 * t + 1:2 * t + 2], self.rt[par][:])
                    v.drain()
                    ins = v.tensor_scalar(at[par][:], xa[par][:],
                                          self.lnsave[:, 2 * t:2 * t + 1],
                                          self.lnsave[:, 2 * t + 1:2 * t + 2],
                                          ALU.subtract, ALU.mult)
                    nt['v'] = self.inc(ins, 'dv', eng=v)
                self.blk.vector(fvB)
                guard_xa[par] = nt['v']

                def fst(sp, t=t, par=par):
                    sp.wait_ge(self.S['dv'], nt['v'])
                    self.dma(sp, f'ats{par}', self.at_s[128 * t:128 * (t + 1), :],
                             at[par][:])
                self.blk.scalar(fst)
                guard_at[par] = self.n[f'ats{par}']
                tok_st = self.n[f'ats{par}']

                def fldT(sq, t=t, par=par, tok_st=tok_st,
                         g=(e0tok[t - 2] if t >= 2 else None)):
                    sq.wait_ge(self.S[f'ats{par}'], tok_st)
                    if g is not None:
                        sq.wait_ge(self.S['pe'], g)
                    self.dma(sq, f'atl{par}',
                             aTv[:, :, 128 * t:128 * (t + 1)],
                             self.at_s[128 * t:128 * (t + 1), :].rearrange(
                                 "t (c p) -> t c p", p=128), transpose=True)
                    aTl_tok[t] = self.n[f'atl{par}']
                self.blk.sync(fldT)

            w_ready = [None, None]
            mm_done = [None, None]
            store_done = [None, None]
            wcst_tok = self.n['wcst']

            def wsl_load(os):
                par = os % 2

                def f(sq, os=os, par=par, g=mm_done[par]):
                    sq.wait_ge(self.S['wcst'], wcst_tok)
                    if g is not None:
                        sq.wait_ge(self.S['pe'], g)
                    self.dma(sq, f'ws{par}', wslv[par],
                             self.watb_s[512 * os:512 * (os + 1), :].rearrange(
                                 "o (c p) -> o c p", p=128), transpose=True)
                    w_ready[par] = self.n[f'ws{par}']
                self.blk.sync(f)

            psq = [self.psA, self.psB, self.psC, self.psD]
            ps_guard = [None] * 4

            def e_tile(os, t):
                par = os % 2
                par4 = t % 4
                pmm = psq[par4]

                def fmm(te, os=os, t=t, par=par, pmm=pmm, wt=w_ready[par],
                        g=ps_guard[par4], al=aTl_tok[t], apar=t % 2):
                    te.wait_ge(self.S[f'ws{par}'], wt)
                    te.wait_ge(self.S[f'atl{apar}'], al)
                    if g is not None:
                        te.wait_ge(self.S['dv'], g)
                    for c in range(ND):
                        ins = te.matmul(pmm[:], aTv[:, c, 128 * t:128 * (t + 1)],
                                        wslv[par][:, c, :],
                                        start=(c == 0), stop=(c == ND - 1))
                    self.inc(ins, 'pe')
                self.blk.tensor(fmm)
                tok_mm = self.n['pe']
                if t == self.NT - 1:
                    mm_done[par] = tok_mm
                if os == 0:
                    e0tok[t] = tok_mm
                epar = t % 2
                nt = {}

                def fv(v, os=os, pmm=pmm, epar=epar, tok_mm=tok_mm,
                       g=store_done[epar]):
                    v.wait_ge(self.S['pe'], tok_mm)
                    if g is not None:
                        v.wait_ge(self.S[f'qks{epar}'], g)
                    if os < 4:
                        ins = v.tensor_scalar(qkvb[epar][:], pmm[:], 1.0 / 128.0,
                                              None, ALU.mult)
                    else:
                        ins = v.tensor_copy(qkvb[epar][:], pmm[:])
                    nt['e'] = self.inc(ins, 'dv', eng=v)
                self.blk.vector(fv)
                ps_guard[par4] = nt['e']

                def fst(sp, os=os, t=t, epar=epar, tok=nt['e']):
                    sp.wait_ge(self.S['dv'], tok)
                    self.dma(sp, f'qks{epar}',
                             self.qkv_s[128 * t:128 * (t + 1),
                                        512 * os:512 * (os + 1)],
                             qkvb[epar][:])
                self.blk.scalar(fst)
                store_done[epar] = self.n[f'qks{epar}']

            for t in range(NT):
                if 'pd' in SERIAL:
                    self.quiesce()
                stageD(t)
                if t == 0:
                    wsl_load(0)
                if t >= 1:
                    e_tile(0, t - 1)
            e_tile(0, NT - 1)
            wsl_load(1)
            for os in range(1, 12):
                for t in range(NT):
                    if 'pq' in SERIAL:
                        self.quiesce()
                    if t == 0 and os + 1 < 12:
                        wsl_load(os + 1)
                    e_tile(os, t)
        self.quiesce()

    def p_attn_win(self, x_in, x_out, out_sem):
        NT, T = self.NT, self.T
        with ExitStack() as es:
            oT = self.sb("oT", [128, ND * T], BF16, es)
            oTv = oT[:].rearrange("p (c t) -> p c t", t=T)
            kw = [self.sb(f"kw{i}", [128, ND * 128], BF16, es) for i in range(2)]
            qw = [self.sb(f"qw{i}", [128, ND * 128], BF16, es) for i in range(2)]
            vw = [self.sb(f"vw{i}", [128, D], BF16, es) for i in range(2)]
            kwv = [k[:].rearrange("p (c j) -> p c j", j=128) for k in kw]
            qwv = [k[:].rearrange("p (c j) -> p c j", j=128) for k in qw]
            rpc = self.sb("rpc", [128, 256], F32, es)
            at2 = self.sb("at2", [128, 256], BF16, es)

            kv_tok = [None, None]
            last_av = [None]
            relu_guard = [None]
            mask_guard = [None]
            evac_guard = [None, None]
            qks_tot = (self.n.get('qks0', 0), self.n.get('qks1', 0))

            def fz(v):
                v.memset(rpc[:], 0.0)
                ins = v.drain()
                self.inc(ins, 'dv')
            self.blk.vector(fz)

            def loadw(w):
                par = w % 2

                def f(sq, w=w, par=par, g=last_av[0]):
                    sq.wait_ge(self.S['qks0'], qks_tot[0])
                    sq.wait_ge(self.S['qks1'], qks_tot[1])
                    if g is not None:
                        sq.wait_ge(self.S['pe'], g)
                    row = self.qkv_s[128 * w:128 * (w + 1), :]
                    self.dma(sq, f'kv{par}', qwv[par],
                             row[:, 0:D].rearrange("t (c p) -> t c p", p=128),
                             transpose=True)
                    self.dma(sq, f'kv{par}', kwv[par],
                             row[:, D:2 * D].rearrange("t (c p) -> t c p", p=128),
                             transpose=True)
                    self.dma(sq, f'kv{par}', vw[par][:], row[:, 2 * D:3 * D])
                    kv_tok[par] = self.n[f'kv{par}']
                self.blk.sync(f)

            loadw(0)
            for w in range(NT):
                if 'pw' in SERIAL:
                    self.quiesce()
                par = w % 2
                havep = w > 0

                def fsim(te, w=w, par=par, havep=havep, rg=relu_guard[0],
                         kt=kv_tok[par]):
                    te.wait_ge(self.S[f'kv{par}'], kt)
                    if rg is not None:
                        te.wait_ge(self.S['ac'], rg)
                    if havep:
                        for c in range(ND):
                            te.matmul(self.psS1[:, 0:128], kwv[1 - par][:, c, :],
                                      qwv[par][:, c, :], start=(c == 0),
                                      stop=(c == ND - 1))
                    for c in range(ND):
                        ins = te.matmul(self.psS2[:, 0:128], kwv[par][:, c, :],
                                        qwv[par][:, c, :], start=(c == 0),
                                        stop=(c == ND - 1))
                    self.inc(ins, 'pe')
                self.blk.tensor(fsim)
                tok_sim = self.n['pe']

                def frelu(s, havep=havep, tok_sim=tok_sim, mg=mask_guard[0]):
                    s.wait_ge(self.S['pe'], tok_sim)
                    if mg is not None:
                        s.wait_ge(self.S['dv'], mg)
                    if havep:
                        s.activation(rpc[:, 0:128], self.psS1[:, 0:128], AFT.Relu)
                    ins = s.activation(rpc[:, 128:256], self.psS2[:, 0:128],
                                       AFT.Relu)
                    self.inc(ins, 'ac')
                self.blk.scalar(frelu)
                relu_guard[0] = self.n['ac']
                tok_relu = self.n['ac']
                nt = {}

                def fmask(v, tok_relu=tok_relu, avg=last_av[0]):
                    v.wait_ge(self.S['ac'], tok_relu)
                    if avg is not None:
                        v.wait_ge(self.S['pe'], avg)
                    v.tensor_tensor(rpc[:], rpc[:], self.maskpc[:], ALU.mult)
                    v.drain()
                    ins = v.tensor_tensor(at2[:], rpc[:], rpc[:], ALU.mult)
                    nt['m'] = self.inc(ins, 'dv', eng=v)
                self.blk.vector(fmask)
                mask_guard[0] = nt['m']
                tok_at = nt['m']

                for grp in range(4):
                    psv = (self.psV1, self.psV2)[grp % 2]
                    gpar = grp % 2

                    def fav(te, par=par, havep=havep, grp=grp, psv=psv,
                            tok_at=tok_at, ev=evac_guard[grp % 2]):
                        te.wait_ge(self.S['dv'], tok_at)
                        if ev is not None:
                            te.wait_ge(self.S['dv'], ev)
                        for j in range(4):
                            c = 4 * grp + j
                            if havep:
                                te.matmul(psv[:, 128 * j:128 * (j + 1)],
                                          vw[1 - par][:, 128 * c:128 * (c + 1)],
                                          at2[:, 0:128], start=True, stop=False)
                            ins = te.matmul(psv[:, 128 * j:128 * (j + 1)],
                                            vw[par][:, 128 * c:128 * (c + 1)],
                                            at2[:, 128:256], start=(not havep),
                                            stop=True)
                        self.inc(ins, 'pe')
                    self.blk.tensor(fav)
                    tok_av = self.n['pe']
                    if grp == 3:
                        last_av[0] = tok_av

                    def fve(v, w=w, grp=grp, psv=psv, tok_av=tok_av):
                        v.wait_ge(self.S['pe'], tok_av)
                        ins = v.tensor_copy(
                            oTv[:, 4 * grp:4 * grp + 4, 128 * w:128 * (w + 1)],
                            psv[:].rearrange("p (j tt) -> p j tt", tt=128))
                        self.inc(ins, 'dv', eng=v)
                    self.blk.vector(fve)
                    evac_guard[gpar] = self.n['dv']
                if w + 1 < NT:
                    loadw(w + 1)
            tok_oT = self.n['dv']

            wob = [self.sb(f"wob{i}", [128, ND * 512], BF16, es) for i in range(2)]
            wobv = [w[:].rearrange("p (c o) -> p c o", o=512) for w in wob]
            xpc = [self.sb(f"xpg{i}", [128, 512], F32, es) for i in range(2)]
            xo = [self.sb(f"xog{i}", [128, 512], F32, es) for i in range(2)]
            wo_ready = [None, None]
            wo_guard = [None, None]
            xo_guard = [None, None]
            xpc_guard = [None, None]
            psg = [None, None]
            wcst_tok = self.n['wcst']

            def wo_load(os):
                par = os % 2

                def f(sq, os=os, par=par, g=wo_guard[par]):
                    sq.wait_ge(self.S['wcst'], wcst_tok)
                    if g is not None:
                        sq.wait_ge(self.S['pe'], g)
                    self.dma(sq, f'wo{par}', wobv[par],
                             self.wob_s[512 * os:512 * (os + 1), :].rearrange(
                                 "o (c p) -> o c p", p=128), transpose=True)
                    wo_ready[par] = self.n[f'wo{par}']
                self.blk.sync(f)

            wo_load(0)
            for os in range(4):
                par = os % 2
                if os + 1 < 4:
                    wo_load(os + 1)
                for t in range(NT):
                    if 'pg' in SERIAL:
                        self.quiesce()
                    pmm = (self.psA, self.psB)[t % 2]
                    ppar = t % 2

                    def fsx(sq, os=os, t=t, ppar=ppar, g=xpc_guard[ppar]):
                        if g is not None:
                            sq.wait_ge(self.S['dv'], g)
                        self.dma(sq, f'xp{ppar}', xpc[ppar][:],
                                 x_in[128 * t:128 * (t + 1),
                                      512 * os:512 * (os + 1)])
                    self.blk.sync(fsx)
                    tok_xp = self.n[f'xp{ppar}']

                    def fmm(te, os=os, t=t, pmm=pmm, par=par, wt=wo_ready[par],
                            g=psg[ppar]):
                        te.wait_ge(self.S[f'wo{par}'], wt)
                        if os == 0 and t == 0:
                            te.wait_ge(self.S['dv'], tok_oT)
                        if g is not None:
                            te.wait_ge(self.S['dv'], g)
                        for c in range(ND):
                            ins = te.matmul(pmm[:], oTv[:, c, 128 * t:128 * (t + 1)],
                                            wobv[par][:, c, :],
                                            start=(c == 0), stop=(c == ND - 1))
                        self.inc(ins, 'pe')
                    self.blk.tensor(fmm)
                    tok_mm = self.n['pe']
                    if t == NT - 1:
                        wo_guard[par] = tok_mm
                    nt = {}

                    def fvu(v, pmm=pmm, ppar=ppar, tok_mm=tok_mm, tok_xp=tok_xp,
                            g=xo_guard[ppar]):
                        v.wait_ge(self.S['pe'], tok_mm)
                        v.wait_ge(self.S[f'xp{ppar}'], tok_xp)
                        if g is not None:
                            v.wait_ge(self.S[f'{out_sem}{ppar}'], g)
                        ins = v.tensor_tensor(xo[ppar][:], pmm[:], xpc[ppar][:],
                                              ALU.add)
                        nt['u'] = self.inc(ins, 'dv', eng=v)
                    self.blk.vector(fvu)
                    psg[ppar] = nt['u']
                    xpc_guard[ppar] = nt['u']

                    def fso(sp, os=os, t=t, ppar=ppar, tok=nt['u']):
                        sp.wait_ge(self.S['dv'], tok)
                        self.dma(sp, f'{out_sem}{ppar}',
                                 x_out[128 * t:128 * (t + 1),
                                       512 * os:512 * (os + 1)],
                                 xo[ppar][:])
                        xo_guard[ppar] = self.n[f'{out_sem}{ppar}']
                    self.blk.scalar(fso)
        self.quiesce()

    def build(self):
        self.p0()
        self.p_ffn_ab(True, self.x0)
        self.p_ffn_c(True, self.x0, self.x1s, 'p')
        self.p_attn_qkv(self.x1s)
        self.p_attn_win(self.x1s, self.x2s, 'o')
        self.p_ffn_ab(False, self.x2s)
        self.p_ffn_c(False, self.x2s, self.outd, 'f')
        snap = dict(self.n)

        def fin(sq, snap=snap):
            for s, v in snap.items():
                if v:
                    sq.wait_ge(self.S[s], v)
        self.blk.sync(fin)

    def finish(self):
        self.es.close()
        return self.nc


def build_program(T=2048, debug=False):
    mk = MK(T, debug=debug)
    mk.build()
    return mk.finish()


def host_prep(x_full, w_attn, w_o, ln_w, ln_b, W1, b1, W2, b2):
    f = np.float32
    bf = ml_dtypes.bfloat16
    common = {
        "wat": np.ascontiguousarray(w_attn.astype(f, copy=False)),
        "wo": np.ascontiguousarray(w_o.astype(f, copy=False)),
        "w1": np.ascontiguousarray(W1.astype(f, copy=False)),
        "w2": np.ascontiguousarray(W2.astype(f, copy=False)),
        "lnw128": np.ascontiguousarray(np.broadcast_to(ln_w[None, :], (128, D)).astype(f)),
        "lnb128": np.ascontiguousarray(np.broadcast_to(ln_b[None, :], (128, D)).astype(f)),
        "b1v": np.ascontiguousarray(b1[None, :].astype(f)),
        "b2v": np.ascontiguousarray(b2[None, :].astype(f)),
        "ones2d": np.ones((2, 128), dtype=bf),
        "maskpcd": np.ascontiguousarray(np.hstack([
            np.tril(np.ones((128, 128), dtype=f)),
            np.triu(np.ones((128, 128), dtype=f))])),
        "epscd": np.full((128, 1), EPS, dtype=f),
    }
    xs = [np.ascontiguousarray(x_full[i].astype(f, copy=False))
          for i in range(x_full.shape[0])]
    return common, xs


def kernel(L0_attn_x, L0_attn_w_attn, L0_attn_w_o, L0_ffn_ln_w, L0_ffn_ln_b,
           L0_ffn_W1, L0_ffn_b1, L0_ffn_W2, L0_ffn_b2):
    x = np.asarray(L0_attn_x)
    B, T, _ = x.shape
    common, xs = host_prep(x, np.asarray(L0_attn_w_attn), np.asarray(L0_attn_w_o),
                           np.asarray(L0_ffn_ln_w), np.asarray(L0_ffn_ln_b),
                           np.asarray(L0_ffn_W1), np.asarray(L0_ffn_b1),
                           np.asarray(L0_ffn_W2), np.asarray(L0_ffn_b2))
    nc = build_program(T)
    in_maps = [dict(common, x0=xs[i]) for i in range(B)]
    res = run_bass_kernel_spmd(nc, in_maps, core_ids=list(range(B)))
    return np.stack([np.asarray(r["out"]) for r in res.results], axis=0)
